# version 09
# baseline (speedup 1.0000x reference)
import numpy as np
import concourse.bass as bass
import concourse.mybir as mybir
from concourse.bass_utils import run_bass_kernel_spmd
from concourse.alu_op_type import AluOpType as ALU
from contextlib import ExitStack

F32 = mybir.dt.float32
BF16 = mybir.dt.bfloat16
AF = mybir.ActivationFunctionType
AX = mybir.AxisListType

NCORE = 8
SEQ, D, INTER = 8192, 2048, 8192
TPC = SEQ // NCORE
NT = 9
NTO = 8
EPS = 1e-5
C2048 = float(1.0 / (64.0 * np.sqrt(np.float32(2048.0))))
C8192 = float(1.0 / (64.0 * np.sqrt(np.float32(8192.0))))
NEG = -1.0e30
ISCALE = float(1.0 / np.sqrt(np.float32(128.0)))

ENGS = ("tensor", "vector", "scalar", "gpsimd", "sync")
DMA_POOL = 10


def _had(n):
    b = np.arange(n)
    pc = np.zeros((n, n), np.int64)
    x = b[:, None] & b[None, :]
    while x.any():
        pc += x & 1
        x >>= 1
    return np.where(pc % 2 == 0, 1.0, -1.0).astype(np.float32)


class Op:
    __slots__ = ("eng", "fn", "deps", "need_inc", "idx", "is_dma", "dma_slot",
                 "dma_val", "sem_val", "tag")

    def __init__(self, eng, fn, is_dma, tag):
        self.eng, self.fn, self.is_dma, self.tag = eng, fn, is_dma, tag
        self.deps = []
        self.need_inc = False
        self.dma_slot = None
        self.dma_val = None
        self.sem_val = None


class Builder:
    def __init__(self):
        self.ops = []
        self.eng_ops = {e: [] for e in ENGS}
        self.last_w = {}
        self.readers = {}
        self.dma_count = {"gpsimd": 0, "sync": 0, "scalar": 0, "vector": 0, "tensor": 0}
        self.dma_slot_last = {}

    def emit(self, eng, fn, reads=(), writes=(), dma=False, tag=""):
        op = Op(eng, fn, dma, tag)
        deps = set()
        for t in reads:
            w = self.last_w.get(t)
            if w is not None:
                deps.add(w)
        for t in writes:
            w = self.last_w.get(t)
            if w is not None:
                deps.add(w)
            for r in self.readers.get(t, ()):
                deps.add(r)
        deps.discard(op)
        if dma:
            idx = self.dma_count[eng]
            self.dma_count[eng] = idx + 1
            slot = idx % DMA_POOL
            op.dma_slot = (eng, slot)
            op.dma_val = 16 * (idx // DMA_POOL + 1)
            prev = self.dma_slot_last.get((eng, slot))
            if prev is not None:
                deps.add(prev)
            self.dma_slot_last[(eng, slot)] = op
        op.deps = list(deps)
        for t in writes:
            self.last_w[t] = op
            self.readers[t] = []
        for t in reads:
            self.readers.setdefault(t, []).append(op)
        self.ops.append(op)
        self.eng_ops[eng].append(op)
        return op

    def finalize(self, nc, ctx):
        for op in self.ops:
            for d in op.deps:
                d.need_inc = True
        cnt = {e: 0 for e in ENGS}
        for e in ENGS:
            for op in self.eng_ops[e]:
                if op.is_dma:
                    continue
                if op.need_inc:
                    cnt[e] += 1
                    op.sem_val = cnt[e]
        sems = {}
        for e in ENGS:
            sems[("eng", e)] = ctx.enter_context(nc.semaphore(f"se_{e}"))
        for q in ("gpsimd", "sync", "scalar"):
            for s in range(DMA_POOL):
                if self.dma_count[q] > s:
                    sems[("dma", q, s)] = ctx.enter_context(nc.semaphore(f"sd_{q}{s}"))
        self.sems = sems

    def replay(self, eng, h):
        waited = {}
        for op in self.eng_ops[eng]:
            wants = {}
            for d in op.deps:
                if d.is_dma:
                    key = ("dma",) + d.dma_slot
                    val = d.dma_val
                else:
                    key = ("eng", d.eng)
                    val = d.sem_val
                    assert val is not None
                if wants.get(key, -1) < val:
                    wants[key] = val
            for key, val in sorted(wants.items(), key=lambda kv: repr(kv[0])):
                if waited.get(key, -1) >= val:
                    continue
                h.wait_ge(self.sems[key], val)
                waited[key] = val
            inst = op.fn(h)
            if op.is_dma:
                inst.then_inc(self.sems[("dma",) + op.dma_slot], 16)
            elif op.need_inc:
                inst.then_inc(self.sems[("eng", op.eng)], 1)


class Arena:
    PAGE = 1024

    def __init__(self, nc, ctx, nbytes, name="AR"):
        self.t = ctx.enter_context(nc.sbuf_tensor(name, [128, nbytes], mybir.dt.uint8))
        self.nbytes = nbytes

    def buf(self, ofs, nbytes):
        assert ofs % 32 == 0 and ofs + nbytes <= self.nbytes, (ofs, nbytes)
        return Buf(self, ofs, nbytes)


class Buf:
    def __init__(self, arena, ofs, nbytes):
        self.arena, self.ofs, self.nbytes = arena, ofs, nbytes
        p0 = ofs // Arena.PAGE
        p1 = (ofs + nbytes - 1) // Arena.PAGE
        self.tokens = tuple(f"sb{p}" for p in range(p0, p1 + 1))

    def ap(self, dtype, shape=None):
        a = self.arena.t[:, self.ofs:self.ofs + self.nbytes].bitcast(dtype)
        if shape is not None:
            assert shape[0] == 128, shape
            n = 1
            for s in shape[1:]:
                n *= s
            assert n * _dt_size(dtype) <= self.nbytes, (shape, dtype, self.nbytes)
            a = self.arena.t[:, self.ofs:self.ofs + n * _dt_size(dtype)].bitcast(dtype)
            if len(shape) > 2:
                spec = " ".join(f"d{i}" for i in range(1, len(shape)))
                a = a.rearrange(f"p ({spec}) -> p {spec}",
                                **{f"d{i}": shape[i] for i in range(1, len(shape))})
        return a

    def sub(self, ofs, nbytes):
        return Buf(self.arena, self.ofs + ofs, nbytes)


def _dt_size(dt):
    return {F32: 4, BF16: 2}[dt]


def host_prep(inp):
    import ml_dtypes
    BF = ml_dtypes.bfloat16
    x0 = np.ascontiguousarray(np.asarray(inp["L0_attn_x"], np.float32).reshape(SEQ, D))
    BV = np.asarray(inp["L0_attn_block_values"], np.float32)
    cols = np.asarray(inp["L0_attn_cols"], np.int64)
    rows = np.asarray(inp["L0_attn_rows"], np.int64)

    jobs_per_core = []
    for c in range(NCORE):
        jobs = []
        hb1 = 16 * c - 1 if c > 0 else 15
        for i in range(128):
            r = int(rows[i])
            if 16 * c <= r < 16 * c + 16:
                b = r - 16 * c
                jobs.append((i, b // 2, b % 2))
            if r == 0:
                jobs.append((i, 8, 0))
            if r == hb1:
                jobs.append((i, 8, 1))
        jobs_per_core.append(jobs)
    S = max(len(j) for j in jobs_per_core)
    S += S % 2
    S = max(S, 2)
    S2 = S // 2

    xg = np.zeros((NCORE, S, 64, D), np.float32)
    bvt = np.zeros((NCORE, S, 8, 128, 64), np.float32)
    sel = np.zeros((NCORE, S2, 128, NT * 128), np.float32)
    oc_ar = np.arange(64)
    for c in range(NCORE):
        for j, (i, m, h) in enumerate(jobs_per_core[c]):
            xg[c, j] = x0[64 * cols[i]:64 * cols[i] + 64, :]
            bt = BV[i].transpose(1, 2, 0)
            for kp in range(8):
                bvt[c, j, kp, 0:64, :] = bt[:, 2 * kp, :]
                bvt[c, j, kp, 64:128, :] = bt[:, 2 * kp + 1, :]
            sel[c, j // 2, 64 * (j % 2) + oc_ar, 128 * m + 64 * h + oc_ar] = 1.0

    x0c = np.zeros((NCORE, NT, 128, D), np.float32)
    for c in range(NCORE):
        for m in range(NTO):
            x0c[c, m] = x0[TPC * c + 128 * m: TPC * c + 128 * (m + 1)]
        hb1 = 16 * c - 1 if c > 0 else 15
        x0c[c, 8, 0:64] = x0[0:64]
        x0c[c, 8, 64:128] = x0[64 * hb1:64 * hb1 + 64]

    def allowed(i, j):
        return (j <= i) & ((j < 4) | (i - j < 8))

    masks = np.full((NCORE, NTO, 128, 288), NEG, np.float32)
    f128 = np.arange(128)
    f32v = np.arange(32)
    for c in range(NCORE):
        hb1 = 16 * c - 1 if c > 0 else 15
        for t in range(NTO):
            T = NTO * c + t
            iq = (128 * T + f128)[:, None]
            jk = (128 * T + f128)[None, :]
            masks[c, t, :, 0:128] = np.where(allowed(iq, jk), 0.0, NEG)
            if t == 0:
                if c > 0:
                    jk0 = f128.copy()
                    jk0[64:] = 64 * hb1 + f128[64:] - 64
                    masks[c, t, :, 192:256] = np.where(allowed(iq, jk0[None, 64:]), 0.0, NEG)
            else:
                jk = (128 * (T - 1) + f128)[None, :]
                masks[c, t, :, 128:256] = np.where(allowed(iq, jk), 0.0, NEG)
            if T >= 2:
                jk = f32v[None, :]
                masks[c, t, :, 256:288] = np.where(allowed(iq, jk), 0.0, NEG)

    W_up = np.asarray(inp["L0_ffn_W_up"], np.float32)
    W_gate = np.asarray(inp["L0_ffn_W_gate"], np.float32)
    W_down = np.asarray(inp["L0_ffn_W_down"], np.float32)
    W_ug = np.concatenate([np.ascontiguousarray(W_up.T),
                           np.ascontiguousarray(W_gate.T)], axis=1)
    W_ug = np.ascontiguousarray(W_ug.reshape(16, 128, 32, 512).transpose(2, 1, 0, 3)).reshape(32, 128, 8192)
    W_dn = np.ascontiguousarray(W_down.T)
    W_dn = np.ascontiguousarray(W_dn.reshape(32, 2, 128, 2048).transpose(0, 2, 1, 3)).reshape(32, 128, 4096)

    SU_ug = np.asarray(inp["L0_ffn_SU_ug"], np.float32)
    SU_dp = np.asarray(inp["L0_ffn_SU_dp"], np.float32)
    Wsc_ug = np.asarray(inp["L0_ffn_Wscale_ug"], np.float32)
    Wsc_dp = np.asarray(inp["L0_ffn_Wscale_dp"], np.float32)
    su_ug_rep = np.broadcast_to(SU_ug, (128, 2048)).copy()
    su_dp_rep = np.broadcast_to(SU_dp, (128, INTER)).copy()
    wscu_rep = np.broadcast_to(Wsc_ug[:INTER], (128, INTER)).copy()
    wscg_rep = np.broadcast_to(Wsc_ug[INTER:], (128, INTER)).copy()
    wsc_dp_rep = np.broadcast_to(Wsc_dp, (128, 2048)).copy()

    wqT = np.ascontiguousarray(np.asarray(inp["L1_attn_Wq"], np.float32).T)
    wkT = np.ascontiguousarray(np.asarray(inp["L1_attn_Wk"], np.float32).T)
    wvT = np.ascontiguousarray(np.asarray(inp["L1_attn_Wv"], np.float32).T)
    W_qkv = np.concatenate([wqT, wkT, wvT], axis=1)
    W_qkv = np.ascontiguousarray(W_qkv.reshape(16, 128, 8, 512).transpose(2, 1, 0, 3)).reshape(8, 128, 8192)
    woT = np.ascontiguousarray(np.asarray(inp["L1_attn_Wo"], np.float32).T)
    woT = np.ascontiguousarray(woT.reshape(16, 128, 4, 512).transpose(2, 1, 0, 3)).reshape(4, 128, 8192)
    w_gate1 = np.asarray(inp["L1_ffn_w_gate"], np.float32)
    w_up1 = np.asarray(inp["L1_ffn_w_up"], np.float32)
    w_down1 = np.asarray(inp["L1_ffn_w_down"], np.float32)
    W_gu1 = np.concatenate([w_gate1, w_up1], axis=1)
    W_gu1 = np.ascontiguousarray(W_gu1.reshape(16, 128, 32, 512).transpose(2, 1, 0, 3)).reshape(32, 128, 8192)
    w_down1 = np.ascontiguousarray(w_down1.reshape(32, 2, 128, 2048).transpose(0, 2, 1, 3)).reshape(32, 128, 4096)
    wscu_rep = np.ascontiguousarray(wscu_rep.reshape(128, 16, 512).transpose(1, 0, 2))
    wscg_rep = np.ascontiguousarray(wscg_rep.reshape(128, 16, 512).transpose(1, 0, 2))

    H128 = _had(128)
    I128 = np.eye(128, dtype=np.float32)

    bvt = np.ascontiguousarray(
        bvt.reshape(NCORE, S // 2, 2, 8, 128, 64).transpose(0, 1, 4, 2, 3, 5))

    shared = dict(
        W_ug=W_ug, W_dn=W_dn, W_qkv=W_qkv, woT=woT, W_gu1=W_gu1, w_down1=w_down1,
        su_ug_rep=su_ug_rep, su_dp_rep=su_dp_rep,
        wscu_rep=wscu_rep, wscg_rep=wscg_rep, wsc_dp_rep=wsc_dp_rep,
        cH128=H128.astype(BF), cI128=I128.astype(BF),
    )
    percore = [dict(xg=xg[c], bvt=bvt[c], sel=sel[c], x0c=x0c[c], masks=masks[c])
               for c in range(NCORE)]
    return S, shared, percore


def build_program(S, dbg_level=0):
    S2 = S // 2
    nc = bass.Bass("TRN2", target_bir_lowering=False)
    ctx = ExitStack()

    d_in = {}
    def din(name, shape, dt=F32):
        d_in[name] = nc.dram_tensor(name, list(shape), dt, kind="ExternalInput").ap()
        return d_in[name]

    xg_d = din("xg", (S, 64, D))
    bvt_d = din("bvt", (S2, 128, 2, 8, 64))
    sel_d = din("sel", (S2, 128, NT * 128))
    x0c_d = din("x0c", (NT, 128, D))
    masks_d = din("masks", (NTO, 128, 288))
    W_ug_d = din("W_ug", (32, 128, 16 * 512))
    W_dn_d = din("W_dn", (32, 128, 2 * 2048))
    W_qkv_d = din("W_qkv", (8, 128, 16 * 512))
    woT_d = din("woT", (4, 128, 16 * 512))
    W_gu1_d = din("W_gu1", (32, 128, 16 * 512))
    w_down1_d = din("w_down1", (32, 128, 2 * 2048))
    su_ug_rep_d = din("su_ug_rep", (128, 2048))
    su_dp_rep_d = din("su_dp_rep", (128, INTER))
    wscu_rep_d = din("wscu_rep", (16, 128, 512))
    wscg_rep_d = din("wscg_rep", (16, 128, 512))
    wsc_dp_d = din("wsc_dp_rep", (128, 2048))
    H128_d = din("cH128", (128, 128), BF16)
    I128_d = din("cI128", (128, 128), BF16)

    out_d = nc.dram_tensor("out", [TPC, D], F32, kind="ExternalOutput").ap()
    dbg_d = None
    if dbg_level:
        dbg_d = nc.dram_tensor("dbg", [NT, 128, D], F32, kind="ExternalOutput").ap()
    dbg3_d = None
    if dbg_level == 3:
        dbg3_d = nc.dram_tensor("dbg3", [12, 128, D], F32, kind="ExternalOutput").ap()

    PS = [ctx.enter_context(nc.psum_tensor(f"ps{i}", [128, 512], F32))
          for i in range(6)]
    psT = ctx.enter_context(nc.psum_tensor("psT", [128, 1024], BF16))
    psU = ctx.enter_context(nc.psum_tensor("psU", [128, 1024], BF16))

    def ps_ap(i, width=512):
        return PS[i][:, 0:width]

    def ps_tok(i):
        return (f"ps{i}",)

    B = Builder()
    AR = Arena(nc, ctx, 204 * 1024)
    ofs = 0
    def alloc(nbytes, align=64):
        nonlocal ofs
        ofs = (ofs + align - 1) // align * align
        b = AR.buf(ofs, nbytes)
        ofs += nbytes
        return b

    b_const = alloc(2 * 256)
    b_wscdp2 = alloc(8 * 1024)
    b_stats = alloc(2048)
    b_junk = alloc(4096)
    b_masks = alloc(NTO * 288 * 4)
    STREAM = alloc(NT * D * 2)
    OALL = alloc(108 * 1024)
    WBUF = alloc(33 * 1024)
    assert ofs <= 204 * 1024, ofs

    consts = {
        "I128": (b_const.sub(0, 256), I128_d),
        "H128": (b_const.sub(256, 256), H128_d),
    }

    def cap(name):
        return consts[name][0].ap(BF16, (128, 128))

    def dma(queue, out_ap, in_ap, reads=(), writes=(), tag=""):
        return B.emit(queue, lambda h, o=out_ap, i=in_ap: h.dma_start(out=o, in_=i),
                      reads=reads, writes=writes, dma=True, tag=tag)

    def mm(out_ap, lhsT, rhs, start, stop, reads=(), writes=(), tp=None, tag=""):
        return B.emit("tensor",
                      lambda h, o=out_ap, l=lhsT, r=rhs, s1=start, s2=stop, t=tp:
                          h.matmul(o, l, r, start=s1, stop=s2, tile_position=t),
                      reads=reads, writes=writes, tag=tag)

    def ve(fn, reads=(), writes=(), tag=""):
        return B.emit("vector", fn, reads=reads, writes=writes, tag=tag)

    def sc(fn, reads=(), writes=(), tag=""):
        return B.emit("scalar", fn, reads=reads, writes=writes, tag=tag)

    def x_tile(buf, m):
        return buf.sub(m * 4096, 4096)

    def emit_ln(rb, ob, out_dtype=BF16, tag="ln"):
        rap = rb.ap(BF16, (128, 2048))
        st = b_stats
        sm = st.ap(F32, (128, 512))
        ve(lambda h: h.reduce_sum(sm[:, 0:1], rap, axis=AX.X),
           reads=rb.tokens, writes=st.tokens, tag=tag + "_s")
        sc(lambda h: h.activation(out=b_junk.ap(BF16, (128, 2048)), in_=rap, func=AF.Square,
                                  accum_out=sm[:, 1:2]),
           reads=rb.tokens, writes=st.tokens + b_junk.tokens, tag=tag + "_sq")
        ve(lambda h: h.tensor_scalar(sm[:, 2:3], sm[:, 0:1], 1.0 / 2048.0, None, ALU.mult),
           reads=st.tokens, writes=st.tokens, tag=tag + "_m")
        ve(lambda h: h.tensor_tensor(sm[:, 3:4], sm[:, 2:3], sm[:, 2:3], ALU.mult),
           reads=st.tokens, writes=st.tokens)
        ve(lambda h: h.tensor_scalar(sm[:, 4:5], sm[:, 1:2], 1.0 / 2048.0, EPS, ALU.mult, ALU.add),
           reads=st.tokens, writes=st.tokens)
        ve(lambda h: h.tensor_tensor(sm[:, 4:5], sm[:, 4:5], sm[:, 3:4], ALU.subtract),
           reads=st.tokens, writes=st.tokens)
        sc(lambda h: h.activation(out=sm[:, 5:6], in_=sm[:, 4:5], func=AF.Sqrt),
           reads=st.tokens, writes=st.tokens, tag=tag + "_rt")
        ve(lambda h: h.reciprocal(sm[:, 6:7], sm[:, 5:6]),
           reads=st.tokens, writes=st.tokens)
        ve(lambda h: h.tensor_tensor(sm[:, 7:8], sm[:, 2:3], sm[:, 6:7], ALU.mult),
           reads=st.tokens, writes=st.tokens)
        ve(lambda h: h.tensor_scalar(sm[:, 7:8], sm[:, 7:8], -1.0, None, ALU.mult),
           reads=st.tokens, writes=st.tokens)
        oap = ob.ap(out_dtype, (128, 2048))
        sc(lambda h: h.activation(out=oap, in_=rap, func=AF.Identity,
                                  scale=sm[:, 6:7], bias=sm[:, 7:8]),
           reads=rb.tokens + st.tokens, writes=ob.tokens, tag=tag + "_n")

    for name, (b, dap) in consts.items():
        dma("gpsimd", b.ap(BF16, (128, 128)), dap, writes=b.tokens, tag=f"ld_{name}")
    dma("gpsimd", b_wscdp2.ap(F32, (128, 2048)), d_in["wsc_dp_rep"], writes=b_wscdp2.tokens)
    for t in range(NTO):
        dma("gpsimd", b_masks.ap(F32, (128, NTO, 288))[:, t, :], d_in["masks"][t],
            writes=b_masks.tokens, tag=f"ld_mask{t}")
    ve(lambda h: h.tensor_scalar_mul(b_wscdp2.ap(F32, (128, 2048)), b_wscdp2.ap(F32, (128, 2048)), 64.0),
       reads=b_wscdp2.tokens, writes=b_wscdp2.tokens)

    def park(name, src_d, nslab, wcols):
        pd = nc.dram_tensor(name, [nslab, 128, wcols], BF16, kind="Internal").ap()
        for cs2 in range(nslab):
            dma("gpsimd", pd[cs2], src_d[cs2], writes=(name,), tag=f"pk_{name}{cs2}")
        return pd

    P_ug = park("P_ug", W_ug_d, 32, 8192)

    CAP = 12
    XGW = 2080
    xga = WBUF.sub(0, XGW * 2)
    xgb = WBUF.sub(XGW * 2, XGW * 2)
    bvt_db = [WBUF.sub(8320, 2048), WBUF.sub(8320 + 2048, 2048)]
    x0_db = [WBUF.sub(17024, 4096), WBUF.sub(17024 + 4096, 4096)]
    dbgst = WBUF.sub(25216, 8192)
    sel_rb = [OALL.sub(CAP * 4096 + k * 2304, 2304) for k in range(CAP)]

    for xb in (xga, xgb):
        ve(lambda h, a=xb.ap(BF16, (128, XGW)): h.memset(a, 0.0), writes=xb.tokens, tag="xgz")

    n_rounds = (S2 + CAP - 1) // CAP
    for rnd in range(n_rounds):
        q0, q1 = rnd * CAP, min(S2, (rnd + 1) * CAP)
        for q in range(q0, q1):
            sb = sel_rb[q - q0]
            dma("gpsimd", sb.ap(BF16, (128, 1152)), sel_d[q],
                reads=sb.tokens, writes=sb.tokens, tag=f"sel{q}")
            bb = bvt_db[q % 2]
            dma("gpsimd", bb.ap(BF16, (128, 2, 8, 64)), bvt_d[q],
                reads=bb.tokens, writes=bb.tokens, tag=f"bvt{q}")
            for half, s in ((0, 2 * q), (1, 2 * q + 1)):
                xb = xga if half == 0 else xgb
                dma("gpsimd", xb.ap(BF16, (128, XGW))[0:64, 15:15 + 2048], xg_d[s],
                    reads=xb.tokens, writes=xb.tokens, tag=f"xgA{s}")
                dma("gpsimd", xb.ap(BF16, (128, XGW))[64:128, 14:14 + 2048], xg_d[s],
                    reads=xb.tokens, writes=xb.tokens, tag=f"xgB{s}")
            ob = OALL.sub((q - q0) * 4096, 4096)
            for j in range(4):
                pis = ((q % 2) * 3 + (j % 2), (q % 2) * 3 + 2)
                for half, xb in ((0, xga), (1, xgb)):
                    pi = pis[half]
                    pap = ps_ap(pi)
                    for kp in range(8):
                        lhsT = bb.ap(BF16, (128, 2, 8, 64))[:, half, kp, :]
                        rhs = xb.ap(BF16, (128, XGW))[:, 2 * kp + 512 * j: 2 * kp + 512 * j + 512]
                        mm(pap[64 * half:64 * (half + 1), :],
                           lhsT, rhs, start=(kp == 0), stop=(kp == 7),
                           reads=bb.tokens + xb.tokens,
                           writes=ps_tok(pi) if kp in (0, 7) else (),
                           tp=(0, 64 * half), tag=f"cv{q}.{j}.{half}")
                    ve(lambda h, o=ob.ap(BF16, (128, 2048))[64 * half:64 * (half + 1), 512 * j:512 * (j + 1)],
                           p=pap[64 * half:64 * (half + 1), :]:
                           h.tensor_copy(o, p),
                       reads=ps_tok(pi), writes=ob.tokens, tag=f"cvev{q}.{j}.{half}")
        if dbg_level == 3 and rnd == 0:
            for q in range(q0, q1):
                ob = OALL.sub((q - q0) * 4096, 4096)
                ve(lambda h, o=dbgst.ap(F32, (128, 2048)), i=ob.ap(BF16, (128, 2048)):
                       h.tensor_copy(o, i),
                   reads=ob.tokens, writes=dbgst.tokens, tag=f"d3c{q}")
                dma("sync", dbg3_d[q], dbgst.ap(F32, (128, 2048)), reads=dbgst.tokens,
                    writes=(f"dbg3_{q}",), tag=f"d3{q}")
        last_round = (rnd == n_rounds - 1)
        for m in range(NT):
            xb0 = x0_db[m % 2]
            if last_round:
                dma("gpsimd", xb0.ap(BF16, (128, 2048)), x0c_d[m],
                    reads=xb0.tokens, writes=xb0.tokens, tag=f"x0bf{m}")
            for j in range(4):
                pi = (m % 2) * 3 + (j % 2)
                pap = ps_ap(pi)
                for q in range(q0, q1):
                    sb = sel_rb[q - q0]
                    ob = OALL.sub((q - q0) * 4096, 4096)
                    st_f = (q == q0)
                    sp_f = (not last_round and q == q1 - 1)
                    mm(pap, sb.ap(BF16, (128, 1152))[:, 128 * m:128 * (m + 1)],
                       ob.ap(BF16, (128, 2048))[:, 512 * j:512 * (j + 1)],
                       start=st_f, stop=sp_f,
                       reads=sb.tokens + ob.tokens,
                       writes=ps_tok(pi) if (st_f or sp_f) else (), tag=f"sel{m}.{j}")
                if last_round:
                    mm(pap, cap("I128"), xb0.ap(BF16, (128, 2048))[:, 512 * j:512 * (j + 1)],
                       start=False, stop=True,
                       reads=consts["I128"][0].tokens + xb0.tokens, writes=ps_tok(pi),
                       tag=f"selr{m}.{j}")
                rb = x_tile(STREAM, m)
                if rnd == 0:
                    ve(lambda h, o=rb.ap(BF16, (128, 2048))[:, 512 * j:512 * (j + 1)], p=pap:
                           h.tensor_copy(o, p),
                       reads=ps_tok(pi), writes=rb.tokens, tag=f"selev{m}.{j}")
                else:
                    ve(lambda h, o=rb.ap(BF16, (128, 2048))[:, 512 * j:512 * (j + 1)], p=pap:
                           h.tensor_tensor(o, o, p, ALU.add),
                       reads=ps_tok(pi) + rb.tokens, writes=rb.tokens, tag=f"selac{m}.{j}")

    ACT = OALL
    for m in range(NT):
        emit_ln(x_tile(STREAM, m), x_tile(ACT, m), tag=f"ln1.{m}")

    def dump_and_exit(ntile):
        fstage = b_masks.sub(0, 8192)
        fence_tokens = []
        for m in range(ntile):
            xb = x_tile(ACT, m)
            ve(lambda h, o=fstage.ap(F32, (128, 2048)), i=xb.ap(BF16, (128, 2048)):
               h.tensor_copy(o, i),
               reads=xb.tokens, writes=fstage.tokens, tag=f"dbgc{m}")
            dma("sync", dbg_d[m], fstage.ap(F32, (128, 2048)), reads=fstage.tokens,
                writes=(f"dbg{m}",), tag=f"dbg{m}")
            fence_tokens.append(f"dbg{m}")
        B.emit("sync", lambda h: h.nop(), reads=tuple(fence_tokens), tag="fence")
        B.finalize(nc, ctx)
        _replay(nc, B)
        ctx.close()
        return nc

    if dbg_level == 1:
        return dump_and_exit(NT)

    P_dn = park("P_dn", W_dn_d, 32, 4096)
    P_qkv = park("P_qkv", W_qkv_d, 8, 8192)
    P_wo = park("P_wo", woT_d, 4, 8192)
    P_gu1 = park("P_gu1", W_gu1_d, 32, 8192)
    P_dn1 = park("P_dn1", w_down1_d, 32, 4096)

    OALL2 = OALL.sub(36864, 73728)
    Hc = cap("H128")
    Ic = cap("I128")

    def transpose_tile(src_buf_ap, dst_buf, nchunk, tag, np_src=None):
        for r in range((nchunk + 3) // 4):
            pst = psT if r % 2 == 0 else psU
            ptok = ("psT",) if r % 2 == 0 else ("psU",)
            c0, c1 = 4 * r, min(nchunk, 4 * r + 4)
            for c in range(c0, c1):
                B.emit("tensor",
                       lambda h, o=pst[:, (c % 4) * 128:(c % 4) * 128 + 128],
                       i=src_buf_ap[:, c * 128:c * 128 + 128]:
                       h.transpose(o, i, Ic),
                       reads=(np_src or ()) + consts["I128"][0].tokens,
                       writes=ptok, tag=f"{tag}_t{c}")
            ve(lambda h, o=dst_buf.ap(BF16, (128, nchunk * 128))[:, c0 * 128:c1 * 128],
               p=pst[:, 0:(c1 - c0) * 128]: h.tensor_copy(o, p),
               reads=ptok, writes=dst_buf.tokens, tag=f"{tag}_e{r}")

    def butterfly(nch, buf_a, buf_b, tag):
        cur, oth = buf_a, buf_b
        w = nch * 128
        s = nch // 2
        while s >= 1:
            blk = nch // (2 * s)
            si = s * 128
            ia = cur.ap(BF16, (128, blk, 2, si))
            oa = oth.ap(BF16, (128, blk, 2, si))
            ve(lambda h, o=oa[:, :, 0, :], x=ia[:, :, 0, :], y=ia[:, :, 1, :]:
               h.tensor_tensor(o, x, y, ALU.add),
               reads=cur.tokens, writes=oth.tokens, tag=f"{tag}_p{s}")
            ve(lambda h, o=oa[:, :, 1, :], x=ia[:, :, 0, :], y=ia[:, :, 1, :]:
               h.tensor_tensor(o, x, y, ALU.subtract),
               reads=cur.tokens, writes=oth.tokens, tag=f"{tag}_m{s}")
            cur, oth = oth, cur
            s //= 2
        return cur

    su_ug_sb = WBUF.sub(0, 4096)
    su_dp_sb = WBUF.sub(4096, 16384)
    pp_db = [WBUF.sub(20480, 4096), WBUF.sub(24576, 4096)]
    wsc_db = [WBUF.sub(20480, 2048), WBUF.sub(22528, 2048)]
    wdn_db = [WBUF.sub(20480, 4096), WBUF.sub(24576, 4096)]
    tb_db = [WBUF.sub(28672, 1024), WBUF.sub(29696, 1024)]
    gt_t = [WBUF.sub(24576, 2048), WBUF.sub(26624, 2048), WBUF.sub(30720, 2048)]
    wslab_db = [STREAM.sub(0, 16384), STREAM.sub(16384, 16384)]
    r_b = STREAM.sub(32768, 4096)

    dma("gpsimd", su_ug_sb.ap(BF16, (128, 2048)), d_in["su_ug_rep"],
        writes=su_ug_sb.tokens, tag="ld_suug")
    dma("gpsimd", su_dp_sb.ap(BF16, (128, INTER)), d_in["su_dp_rep"],
        writes=su_dp_sb.tokens, tag="ld_sudp")

    C8192P = float(1.0 / np.sqrt(np.float32(8192.0)))

    F0_GROUPS = [[0, 1, 2], [3, 4, 5], [6, 7, 8]]
    xsT = [OALL2.sub(t * 4096, 4096) for t in range(3)]
    xhT = [OALL2.sub(12288 + t * 4096, 4096) for t in range(3)]
    h_b = [OALL2.sub(24576 + t * 16384, 16384) for t in range(3)]
    sg_b = [b_junk.sub(t * 1024, 1024) for t in range(3)]

    for grp in F0_GROUPS:
        for ti, t in enumerate(grp):
            xa = x_tile(ACT, t)
            ve(lambda h, o=pp_db[0].ap(BF16, (128, 2048)), x=xa.ap(BF16, (128, 2048)),
               s=su_ug_sb.ap(BF16, (128, 2048)): h.tensor_tensor(o, x, s, ALU.mult),
               reads=xa.tokens + su_ug_sb.tokens, writes=pp_db[0].tokens, tag=f"f0su{t}")
            fin = butterfly(16, pp_db[0], pp_db[1], f"f0bf{t}")
            transpose_tile(fin.ap(BF16, (128, 2048)), xsT[ti], 16, f"f0T{t}",
                           np_src=fin.tokens)
            for j in range(4):
                pi = j % 2
                mm(ps_ap(pi), Hc, xsT[ti].ap(BF16, (128, 2048))[:, 512 * j:512 * (j + 1)],
                   start=True, stop=True,
                   reads=consts["H128"][0].tokens + xsT[ti].tokens,
                   writes=ps_tok(pi), tag=f"f0H{t}.{j}")
                sc(lambda h, o=xhT[ti].ap(BF16, (128, 2048))[:, 512 * j:512 * (j + 1)],
                   p=ps_ap(pi): h.activation(out=o, in_=p, func=AF.Identity, scale=C2048),
                   reads=ps_tok(pi), writes=xhT[ti].tokens, tag=f"f0He{t}.{j}")
        for s in range(16):
            for half, cs in ((0, 16 + s), (1, s)):
                wb = wslab_db[(2 * s + half) % 2]
                dma("sync", wb.ap(BF16, (128, 16 * 512)), P_ug[cs],
                    reads=wb.tokens + ("P_ug",), writes=wb.tokens, tag=f"f0w{s}.{half}")
                wsb = wsc_db[(2 * s + half) % 2]
                dma("scalar", wsb.ap(F32, (128, 512)),
                    (d_in["wscg_rep"] if half == 0 else d_in["wscu_rep"])[s],
                    reads=wsb.tokens, writes=wsb.tokens, tag=f"f0ws{s}.{half}")
                for ti, t in enumerate(grp):
                    pi = ti
                    for a in range(16):
                        mm(ps_ap(pi), xhT[ti].ap(BF16, (128, 2048))[:, a * 128:(a + 1) * 128],
                           wb.ap(BF16, (128, 16, 512))[:, a, :],
                           start=(a == 0), stop=(a == 15),
                           reads=xhT[ti].tokens + wb.tokens,
                           writes=ps_tok(pi) if a in (0, 15) else (),
                           tag=f"f0m{s}.{half}.{t}")
                    gtb = gt_t[ti]
                    if half == 0:
                        ve(lambda h, o=gtb.ap(F32, (128, 512)), p=ps_ap(pi),
                           w=wsb.ap(F32, (128, 512)): h.tensor_tensor(o, p, w, ALU.mult),
                           reads=ps_tok(pi) + wsb.tokens, writes=gtb.tokens, tag=f"f0g{s}.{t}")
                        sc(lambda h, o=sg_b[ti].ap(BF16, (128, 512)), i=gtb.ap(F32, (128, 512)):
                           h.activation(out=o, in_=i, func=AF.Silu, scale=64.0),
                           reads=gtb.tokens, writes=sg_b[ti].tokens, tag=f"f0sg{s}.{t}")
                    else:
                        ve(lambda h, o=gtb.ap(F32, (128, 512)), p=ps_ap(pi),
                           w=wsb.ap(F32, (128, 512)): h.tensor_tensor(o, p, w, ALU.mult),
                           reads=ps_tok(pi) + wsb.tokens, writes=gtb.tokens, tag=f"f0u{s}.{t}")
                        ve(lambda h, o=h_b[ti].ap(BF16, (128, INTER))[:, 512 * s:512 * (s + 1)],
                           x=gtb.ap(F32, (128, 512)), y=sg_b[ti].ap(BF16, (128, 512)):
                           h.tensor_tensor(o, x, y, ALU.mult),
                           reads=gtb.tokens + sg_b[ti].tokens, writes=h_b[ti].tokens,
                           tag=f"f0h{s}.{t}")
        for ti, t in enumerate(grp):
            hb = h_b[ti]
            ve(lambda h, o=hb.ap(BF16, (128, INTER)), s=su_dp_sb.ap(BF16, (128, INTER)):
               h.tensor_tensor(o, o, s, ALU.mult),
               reads=hb.tokens + su_dp_sb.tokens, writes=hb.tokens, tag=f"f0sd{t}")
            pp8 = STREAM.sub(0, 16384)
            fin = butterfly(64, hb, pp8, f"f0b8{t}")
            xwT = STREAM.sub(16384, 16384)
            for q in range(16):
                pst = psT if q % 2 == 0 else psU
                ptok = ("psT",) if q % 2 == 0 else ("psU",)
                for j in range(4):
                    c = 4 * q + j
                    B.emit("tensor",
                           lambda h, o=pst[:, j * 128:(j + 1) * 128],
                           i=fin.ap(BF16, (128, INTER))[:, c * 128:(c + 1) * 128]:
                           h.transpose(o, i, Ic),
                           reads=fin.tokens + consts["I128"][0].tokens,
                           writes=ptok, tag=f"f0dT{t}.{c}")
                tb = tb_db[q % 2]
                ve(lambda h, o=tb.ap(BF16, (128, 512)), p=pst[:, 0:512]: h.tensor_copy(o, p),
                   reads=ptok, writes=tb.tokens, tag=f"f0dTe{t}.{q}")
                pi = 4 + (q % 2)
                mm(ps_ap(pi), Hc, tb.ap(BF16, (128, 512)), start=True, stop=True,
                   reads=consts["H128"][0].tokens + tb.tokens, writes=ps_tok(pi),
                   tag=f"f0dH{t}.{q}")
                sc(lambda h, o=xwT.ap(BF16, (128, INTER))[:, 512 * q:512 * (q + 1)],
                   p=ps_ap(pi): h.activation(out=o, in_=p, func=AF.Identity, scale=C8192P),
                   reads=ps_tok(pi), writes=xwT.tokens, tag=f"f0dHe{t}.{q}")
            wdn2_db = [hb.sub(0, 8192), hb.sub(8192, 8192), OALL2.sub(12288, 8192)]
            for cp in range(32):
                wd = wdn2_db[cp % 3]
                dma("gpsimd", wd.ap(BF16, (128, 4096)), P_dn[cp],
                    reads=wd.tokens + ("P_dn",), writes=wd.tokens, tag=f"f0dw{t}.{cp}")
                for k in range(2):
                    c = 2 * cp + k
                    for j in range(4):
                        mm(ps_ap(j), xwT.ap(BF16, (128, INTER))[:, c * 128:(c + 1) * 128],
                           wd.ap(BF16, (128, 2, 2048))[:, k, 512 * j:512 * (j + 1)],
                           start=(c == 0), stop=(c == 63),
                           reads=xwT.tokens + wd.tokens,
                           writes=ps_tok(j) if c in (0, 63) else (),
                           tag=f"f0dm{t}.{c}.{j}")
            xa = x_tile(ACT, t)
            rb0 = xsT[ti]
            for j in range(4):
                gtb = gt_t[j % 2]
                ve(lambda h, o=gtb.ap(F32, (128, 512)), p=ps_ap(j),
                   w=b_wscdp2.ap(F32, (128, 2048))[:, 512 * j:512 * (j + 1)]:
                   h.tensor_tensor(o, p, w, ALU.mult),
                   reads=ps_tok(j) + b_wscdp2.tokens, writes=gtb.tokens, tag=f"f0r{t}.{j}")
                ve(lambda h, o=rb0.ap(BF16, (128, 2048))[:, 512 * j:512 * (j + 1)],
                   x=gtb.ap(F32, (128, 512)), y=xa.ap(BF16, (128, 2048))[:, 512 * j:512 * (j + 1)]:
                   h.tensor_tensor(o, x, y, ALU.add),
                   reads=gtb.tokens + xa.tokens, writes=rb0.tokens, tag=f"f0ra{t}.{j}")
            emit_ln(rb0, x_tile(ACT, t), tag=f"ln2.{t}")

    if dbg_level == 4:
        return dump_and_exit(NT)

    Qbuf = [OALL2.sub(t * 4096, 4096) for t in range(2)]
    Kbuf = [OALL2.sub(8192 + t * 2048, 2048) for t in range(2)]
    KT_st = [OALL2.sub(12288 + t * 2048, 2048) for t in range(NT)]
    V_st = [OALL2.sub(30720 + t * 2048, 2048) for t in range(NT)]
    QT = [OALL2.sub(49152 + t * 4096, 4096) for t in range(2)]
    attnT = [OALL2.sub(57344 + t * 4096, 4096) for t in range(2)]
    X2T = [OALL2.sub(65536 + t * 4096, 4096) for t in range(2)]
    _scb = [b_junk, STREAM.sub(32768, 4096)]
    sc_s = [b.sub(0, 1152) for b in _scb]
    sc_e = [b.sub(1152, 576) for b in _scb]
    sc_a = [b.sub(1728, 576) for b in _scb]
    sc_at = [b.sub(2304, 768) for b in _scb]
    sc_r = [b.sub(3072, 32) for b in _scb]
    wA_db = [STREAM.sub(0, 16384), STREAM.sub(16384, 16384)]
    wo_db = [WBUF.sub(0, 16384), WBUF.sub(16384, 16384)]
    rb_of = lambda ti: Qbuf[ti]

    A1_GROUPS = [[8, 0], [1, 2], [3, 4], [5, 6], [7]]

    def prev_tile(t):
        return 8 if t == 0 else t - 1

    for grp in A1_GROUPS:
        for ti, t in enumerate(grp):
            transpose_tile(x_tile(ACT, t).ap(BF16, (128, 2048)), X2T[ti], 16, f"a1xT{t}",
                           np_src=x_tile(ACT, t).tokens)
        for s in range(8):
            wb = wA_db[s % 2]
            dma("scalar", wb.ap(BF16, (128, 16 * 512)), P_qkv[s],
                reads=wb.tokens + ("P_qkv",), writes=wb.tokens, tag=f"a1w{s}")
            for ti, t in enumerate(grp):
                pi = ti % 2
                for a in range(16):
                    mm(ps_ap(pi), X2T[ti].ap(BF16, (128, 2048))[:, a * 128:(a + 1) * 128],
                       wb.ap(BF16, (128, 16, 512))[:, a, :],
                       start=(a == 0), stop=(a == 15),
                       reads=X2T[ti].tokens + wb.tokens,
                       writes=ps_tok(pi) if a in (0, 15) else (),
                       tag=f"a1m{s}.{t}")
                if s < 4:
                    dst, off, scl = Qbuf[ti], 512 * s, ISCALE
                elif s < 6:
                    dst, off, scl = Kbuf[ti], 512 * (s - 4), 1.0
                else:
                    dst, off, scl = V_st[t], 512 * (s - 6), 1.0
                sc(lambda h, o=dst.ap(BF16, (128, dst.nbytes // 2))[:, off:off + 512],
                   p=ps_ap(pi), sv=scl: h.activation(out=o, in_=p, func=AF.Identity, scale=sv),
                   reads=ps_tok(pi), writes=dst.tokens, tag=f"a1e{s}.{t}")
        for ti, t in enumerate(grp):
            transpose_tile(Qbuf[ti].ap(BF16, (128, 2048)), QT[ti], 16, f"a1qT{t}",
                           np_src=Qbuf[ti].tokens)
            transpose_tile(Kbuf[ti].ap(BF16, (128, 1024)), KT_st[t], 8, f"a1kT{t}",
                           np_src=Kbuf[ti].tokens)
        for ti, t in enumerate(grp):
            if t == 8:
                continue
            tp = prev_tile(t)
            mask_ap = b_masks.ap(F32, (128, NTO, 288))[:, t, :]
            for qh in range(16):
                kv = qh // 2
                par = qh % 2
                scs, sce, sca, scat, scr = (sc_s[par], sc_e[par], sc_a[par],
                                            sc_at[par], sc_r[par])
                pst, ptok = (psT, ("psT",)) if par == 0 else (psU, ("psU",))
                pi = 2 + par
                po = 4 + par
                lq = QT[ti].ap(BF16, (128, 2048))[:, qh * 128:(qh + 1) * 128]
                segs = [
                    (KT_st[t].ap(BF16, (128, 1024))[:, kv * 128:(kv + 1) * 128], 0, 128),
                    (KT_st[tp].ap(BF16, (128, 1024))[:, kv * 128:(kv + 1) * 128], 128, 128),
                    (KT_st[8].ap(BF16, (128, 1024))[:, kv * 128:kv * 128 + 32], 256, 32),
                ]
                for (kap, off, w) in segs:
                    mm(PS[pi][:, off:off + w], lq, kap, start=True, stop=True,
                       reads=QT[ti].tokens + KT_st[t].tokens + KT_st[tp].tokens + KT_st[8].tokens,
                       writes=ps_tok(pi), tag=f"a1s{t}.{qh}")
                ve(lambda h, o=scs.ap(F32, (128, 288)), p=PS[pi][:, 0:288], mk=mask_ap:
                   h.tensor_tensor(o, p, mk, ALU.add),
                   reads=ps_tok(pi) + b_masks.tokens, writes=scs.tokens, tag=f"a1sm{t}.{qh}")
                sc(lambda h, o=sce.ap(BF16, (128, 288)), i=scs.ap(F32, (128, 288)),
                   a=scr.ap(F32, (128, 8))[:, 0:1]:
                   h.activation(out=o, in_=i, func=AF.Exp, accum_out=a),
                   reads=scs.tokens, writes=sce.tokens + scr.tokens, tag=f"a1ex{t}.{qh}")
                ve(lambda h, o=scr.ap(F32, (128, 8))[:, 1:2], i=scr.ap(F32, (128, 8))[:, 0:1]:
                   h.reciprocal(o, i),
                   reads=scr.tokens, writes=scr.tokens, tag=f"a1rc{t}.{qh}")
                sc(lambda h, o=sca.ap(BF16, (128, 288)), i=sce.ap(BF16, (128, 288)),
                   sv=scr.ap(F32, (128, 8))[:, 1:2]:
                   h.activation(out=o, in_=i, func=AF.Identity, scale=sv),
                   reads=sce.tokens + scr.tokens, writes=sca.tokens, tag=f"a1aw{t}.{qh}")
                for sgi, (off, w) in enumerate(((0, 128), (128, 128), (256, 32))):
                    B.emit("tensor",
                           lambda h, o=pst[0:w, sgi * 128:sgi * 128 + 128],
                           i=sca.ap(BF16, (128, 288))[:, off:off + w]:
                           h.transpose(o, i, Ic),
                           reads=sca.tokens + consts["I128"][0].tokens,
                           writes=ptok, tag=f"a1at{t}.{qh}.{sgi}")
                ve(lambda h, o=scat.ap(BF16, (128, 384)), p=pst[:, 0:384]: h.tensor_copy(o, p),
                   reads=ptok, writes=scat.tokens, tag=f"a1ae{t}.{qh}")
                vsegs = [
                    (V_st[t].ap(BF16, (128, 1024))[:, kv * 128:(kv + 1) * 128], 0, 128),
                    (V_st[tp].ap(BF16, (128, 1024))[:, kv * 128:(kv + 1) * 128], 1, 128),
                    (V_st[8].ap(BF16, (128, 1024))[0:32, kv * 128:(kv + 1) * 128], 2, 32),
                ]
                for (vap, sgi, w) in vsegs:
                    mm(PS[po][:, 0:128], vap,
                       scat.ap(BF16, (128, 384))[0:w, sgi * 128:sgi * 128 + 128],
                       start=(sgi == 0), stop=(sgi == 2),
                       reads=scat.tokens + V_st[t].tokens + V_st[tp].tokens + V_st[8].tokens,
                       writes=ps_tok(po) if sgi in (0, 2) else (),
                       tag=f"a1o{t}.{qh}.{sgi}")
                ve(lambda h, o=attnT[ti].ap(BF16, (128, 2048))[:, qh * 128:(qh + 1) * 128],
                   p=PS[po][:, 0:128]: h.tensor_copy(o, p),
                   reads=ps_tok(po), writes=attnT[ti].tokens, tag=f"a1oe{t}.{qh}")
        for os_ in range(4):
            wb = wo_db[os_ % 2]
            dma("scalar", wb.ap(BF16, (128, 16 * 512)), P_wo[os_],
                reads=wb.tokens + ("P_wo",), writes=wb.tokens, tag=f"a1wo{os_}")
            for ti, t in enumerate(grp):
                if t == 8:
                    continue
                pi = ti % 2
                for a in range(16):
                    mm(ps_ap(pi), attnT[ti].ap(BF16, (128, 2048))[:, a * 128:(a + 1) * 128],
                       wb.ap(BF16, (128, 16, 512))[:, a, :],
                       start=(a == 0), stop=(a == 15),
                       reads=attnT[ti].tokens + wb.tokens,
                       writes=ps_tok(pi) if a in (0, 15) else (),
                       tag=f"a1om{os_}.{t}")
                xa = x_tile(ACT, t)
                ve(lambda h, o=rb_of(ti).ap(BF16, (128, 2048))[:, 512 * os_:512 * (os_ + 1)],
                   p=ps_ap(pi), x=xa.ap(BF16, (128, 2048))[:, 512 * os_:512 * (os_ + 1)]:
                   h.tensor_tensor(o, p, x, ALU.add),
                   reads=ps_tok(pi) + xa.tokens, writes=rb_of(ti).tokens, tag=f"a1r{os_}.{t}")
        for ti, t in enumerate(grp):
            if t == 8:
                continue
            emit_ln(rb_of(ti), x_tile(ACT, t), tag=f"ln3.{t}")

    if dbg_level == 5:
        return dump_and_exit(NTO)

    h1_b = [OALL2.sub(t * 16384, 16384) for t in range(2)]
    hsT = [OALL2.sub(32768 + t * 16384, 16384) for t in range(2)]
    X3T = [OALL2.sub(65536 + t * 4096, 4096) for t in range(2)]
    A4_b = WBUF.sub(0, 4096)
    mn_b = [WBUF.sub(4096 + i * 1024, 1024) for i in range(4)]
    t2_b = WBUF.sub(8192, 1024)
    msk_b = WBUF.sub(9216, 2048)
    sgf_b = WBUF.sub(11264, 2048)
    wdn1_db = [WBUF.sub(20480, 4096), WBUF.sub(24576, 4096)]
    wg_db = [STREAM.sub(0, 16384), STREAM.sub(16384, 16384)]
    r1_b = STREAM.sub(32768, 4096)
    fstage = b_masks.sub(0, 8192)

    F1_GROUPS = [[0, 1], [2, 3], [4, 5], [6, 7]]
    out_tokens = []
    for grp in F1_GROUPS:
        for ti, t in enumerate(grp):
            transpose_tile(x_tile(ACT, t).ap(BF16, (128, 2048)), X3T[ti], 16, f"f1xT{t}",
                           np_src=x_tile(ACT, t).tokens)
        for s in range(16):
            for half in (0, 1):
                cs = s + half * 16
                wb = wg_db[(2 * s + half) % 2]
                dma("sync" if (2 * s + half) % 2 == 0 else "scalar",
                    wb.ap(BF16, (128, 16 * 512)), P_gu1[cs],
                    reads=wb.tokens + ("P_gu1",), writes=wb.tokens, tag=f"f1w{s}.{half}")
                for ti, t in enumerate(grp):
                    pi = ti % 2
                    for a in range(16):
                        mm(ps_ap(pi), X3T[ti].ap(BF16, (128, 2048))[:, a * 128:(a + 1) * 128],
                           wb.ap(BF16, (128, 16, 512))[:, a, :],
                           start=(a == 0), stop=(a == 15),
                           reads=X3T[ti].tokens + wb.tokens,
                           writes=ps_tok(pi) if a in (0, 15) else (),
                           tag=f"f1m{s}.{half}.{t}")
                    if half == 0:
                        sc(lambda h, o=sg_b[ti].ap(BF16, (128, 512)), p=ps_ap(pi):
                           h.activation(out=o, in_=p, func=AF.Silu),
                           reads=ps_tok(pi), writes=sg_b[ti].tokens, tag=f"f1sg{s}.{t}")
                    else:
                        ve(lambda h, o=h1_b[ti].ap(BF16, (128, INTER))[:, 512 * s:512 * (s + 1)],
                           p=ps_ap(pi), g=sg_b[ti].ap(BF16, (128, 512)):
                           h.tensor_tensor(o, p, g, ALU.mult),
                           reads=ps_tok(pi) + sg_b[ti].tokens, writes=h1_b[ti].tokens,
                           tag=f"f1h{s}.{t}")
        for ti, t in enumerate(grp):
            hb = h1_b[ti]
            for sec in range(4):
                hsec = hb.ap(BF16, (128, INTER))[:, 2048 * sec:2048 * (sec + 1)]
                sc(lambda h, o=A4_b.ap(BF16, (128, 2048)), i=hsec:
                   h.activation(out=o, in_=i, func=AF.Abs),
                   reads=hb.tokens, writes=A4_b.tokens, tag=f"f1ab{t}.{sec}")
                g0 = A4_b.ap(BF16, (128, 512, 4))
                for i, (x, y, op) in enumerate(((0, 1, ALU.max), (0, 1, ALU.min),
                                                (2, 3, ALU.max), (2, 3, ALU.min))):
                    ve(lambda h, o=mn_b[i].ap(BF16, (128, 512)), a1=g0[:, :, x], a2=g0[:, :, y],
                       op_=op: h.tensor_tensor(o, a1, a2, op_),
                       reads=A4_b.tokens, writes=mn_b[i].tokens, tag=f"f1mn{t}.{sec}.{i}")
                ve(lambda h, o=mn_b[0].ap(BF16, (128, 512)), a1=mn_b[0].ap(BF16, (128, 512)),
                   a2=mn_b[2].ap(BF16, (128, 512)): h.tensor_tensor(o, a1, a2, ALU.min),
                   reads=mn_b[0].tokens + mn_b[2].tokens, writes=mn_b[0].tokens,
                   tag=f"f1c1{t}.{sec}")
                ve(lambda h, o=mn_b[1].ap(BF16, (128, 512)), a1=mn_b[1].ap(BF16, (128, 512)),
                   a2=mn_b[3].ap(BF16, (128, 512)): h.tensor_tensor(o, a1, a2, ALU.max),
                   reads=mn_b[1].tokens + mn_b[3].tokens, writes=mn_b[1].tokens,
                   tag=f"f1c2{t}.{sec}")
                ve(lambda h, o=t2_b.ap(BF16, (128, 512)), a1=mn_b[0].ap(BF16, (128, 512)),
                   a2=mn_b[1].ap(BF16, (128, 512)): h.tensor_tensor(o, a1, a2, ALU.max),
                   reads=mn_b[0].tokens + mn_b[1].tokens, writes=t2_b.tokens,
                   tag=f"f1t2{t}.{sec}")
                hv = hb.ap(BF16, (128, INTER)).rearrange(
                    "p (ss g four) -> p ss g four", ss=4, four=4)
                for i in range(4):
                    ve(lambda h, o=msk_b.ap(F32, (128, 512)), a1=g0[:, :, i],
                       a2=t2_b.ap(BF16, (128, 512)): h.tensor_tensor(o, a1, a2, ALU.is_ge),
                       reads=A4_b.tokens + t2_b.tokens, writes=msk_b.tokens,
                       tag=f"f1ge{t}.{sec}.{i}")
                    ve(lambda h, o=hv[:, sec, :, i], mk=msk_b.ap(F32, (128, 512)):
                       h.tensor_tensor(o, o, mk, ALU.mult),
                       reads=hb.tokens + msk_b.tokens, writes=hb.tokens,
                       tag=f"f1ap{t}.{sec}.{i}")
            transpose_tile(hb.ap(BF16, (128, INTER)), hsT[ti], 64, f"f1hT{t}",
                           np_src=hb.tokens)
        wdn3_db = [h1_b[0].sub(0, 8192), h1_b[0].sub(8192, 8192),
                   h1_b[1].sub(0, 8192), h1_b[1].sub(8192, 8192)]
        for half in range(2):
            for cp in range(32):
                wd = wdn3_db[cp % 4]
                dma("gpsimd" if cp % 2 == 0 else "scalar",
                    wd.ap(BF16, (128, 4096)), P_dn1[cp],
                    reads=wd.tokens + ("P_dn1",), writes=wd.tokens,
                    tag=f"f1dw{half}.{cp}")
                for ti, t in enumerate(grp):
                    for k in range(2):
                        c = 2 * cp + k
                        for j in range(2):
                            pi = 2 * ti + j
                            mm(ps_ap(pi),
                               hsT[ti].ap(BF16, (128, INTER))[:, c * 128:(c + 1) * 128],
                               wd.ap(BF16, (128, 2, 2048))[:, k,
                                   1024 * half + 512 * j:1024 * half + 512 * (j + 1)],
                               start=(cp == 0 and k == 0), stop=(cp == 31 and k == 1),
                               reads=hsT[ti].tokens + wd.tokens,
                               writes=ps_tok(pi) if (cp, k) in ((0, 0), (31, 1)) else (),
                               tag=f"f1dm{half}.{t}.{c}.{j}")
            for ti, t in enumerate(grp):
                xa = x_tile(ACT, t)
                rb1 = X3T[ti]
                for j in range(2):
                    jj = 2 * half + j
                    pi = 2 * ti + j
                    ve(lambda h, o=rb1.ap(BF16, (128, 2048))[:, 512 * jj:512 * (jj + 1)],
                       p=ps_ap(pi), x=xa.ap(BF16, (128, 2048))[:, 512 * jj:512 * (jj + 1)]:
                       h.tensor_tensor(o, p, x, ALU.add),
                       reads=ps_tok(pi) + xa.tokens, writes=rb1.tokens, tag=f"f1r{t}.{jj}")
        for ti, t in enumerate(grp):
            rb1 = X3T[ti]
            emit_ln(rb1, fstage, out_dtype=F32, tag=f"ln4.{t}")
            dma("sync", out_d[128 * t:128 * (t + 1), :], fstage.ap(F32, (128, 2048)),
                reads=fstage.tokens, writes=(f"out{t}",), tag=f"out{t}")
            out_tokens.append(f"out{t}")

    B.emit("sync", lambda h: h.nop(), reads=tuple(out_tokens), tag="fence")
    B.finalize(nc, ctx)
    _replay(nc, B)
    ctx.close()
    return nc


def _replay(nc, B):
    with nc.Block() as block:
        @block.tensor
        def _(h):
            B.replay("tensor", h)

        @block.vector
        def _(h):
            B.replay("vector", h)

        @block.scalar
        def _(h):
            B.replay("scalar", h)

        @block.gpsimd
        def _(h):
            B.replay("gpsimd", h)

        @block.sync
        def _(h):
            B.replay("sync", h)


def kernel(**inputs):
    S, shared, percore = host_prep(inputs)
    nc = build_program(S, dbg_level=0)
    in_maps = []
    for c in range(NCORE):
        m = dict(shared)
        m.update(percore[c])
        in_maps.append(m)
    res = run_bass_kernel_spmd(nc, in_maps, core_ids=list(range(NCORE)))
    out = np.concatenate([res.results[c]["out"] for c in range(NCORE)], axis=0)
    return out.reshape(1, SEQ, D).astype(np.float32)


# version 10
# speedup vs baseline: 1.0231x; 1.0231x over previous
import numpy as np
import concourse.bass as bass
import concourse.mybir as mybir
from concourse.bass_utils import run_bass_kernel_spmd
from concourse.alu_op_type import AluOpType as ALU
from contextlib import ExitStack

F32 = mybir.dt.float32
BF16 = mybir.dt.bfloat16
AF = mybir.ActivationFunctionType
AX = mybir.AxisListType

NCORE = 8
SEQ, D, INTER = 8192, 2048, 8192
TPC = SEQ // NCORE
NT = 9
NTO = 8
EPS = 1e-5
C2048 = float(1.0 / (64.0 * np.sqrt(np.float32(2048.0))))
C8192 = float(1.0 / (64.0 * np.sqrt(np.float32(8192.0))))
NEG = -1.0e30
ISCALE = float(1.0 / np.sqrt(np.float32(128.0)))

ENGS = ("tensor", "vector", "scalar", "gpsimd", "sync")
DMA_POOL = 10


def _had(n):
    b = np.arange(n)
    pc = np.zeros((n, n), np.int64)
    x = b[:, None] & b[None, :]
    while x.any():
        pc += x & 1
        x >>= 1
    return np.where(pc % 2 == 0, 1.0, -1.0).astype(np.float32)


class Op:
    __slots__ = ("eng", "fn", "deps", "need_inc", "idx", "is_dma", "dma_slot",
                 "dma_val", "sem_val", "tag")

    def __init__(self, eng, fn, is_dma, tag):
        self.eng, self.fn, self.is_dma, self.tag = eng, fn, is_dma, tag
        self.deps = []
        self.need_inc = False
        self.dma_slot = None
        self.dma_val = None
        self.sem_val = None


class Builder:
    def __init__(self):
        self.ops = []
        self.eng_ops = {e: [] for e in ENGS}
        self.last_w = {}
        self.readers = {}
        self.dma_count = {"gpsimd": 0, "sync": 0, "scalar": 0, "vector": 0, "tensor": 0}
        self.dma_slot_last = {}

    def emit(self, eng, fn, reads=(), writes=(), dma=False, tag=""):
        op = Op(eng, fn, dma, tag)
        deps = set()
        for t in reads:
            w = self.last_w.get(t)
            if w is not None:
                deps.add(w)
        for t in writes:
            w = self.last_w.get(t)
            if w is not None:
                deps.add(w)
            for r in self.readers.get(t, ()):
                deps.add(r)
        deps.discard(op)
        if dma:
            idx = self.dma_count[eng]
            self.dma_count[eng] = idx + 1
            slot = idx % DMA_POOL
            op.dma_slot = (eng, slot)
            op.dma_val = 16 * (idx // DMA_POOL + 1)
            prev = self.dma_slot_last.get((eng, slot))
            if prev is not None:
                deps.add(prev)
            self.dma_slot_last[(eng, slot)] = op
        op.deps = list(deps)
        for t in writes:
            self.last_w[t] = op
            self.readers[t] = []
        for t in reads:
            self.readers.setdefault(t, []).append(op)
        self.ops.append(op)
        self.eng_ops[eng].append(op)
        return op

    def finalize(self, nc, ctx):
        for op in self.ops:
            for d in op.deps:
                d.need_inc = True
        cnt = {e: 0 for e in ENGS}
        for e in ENGS:
            for op in self.eng_ops[e]:
                if op.is_dma:
                    continue
                if op.need_inc:
                    cnt[e] += 1
                    op.sem_val = cnt[e]
        sems = {}
        for e in ENGS:
            sems[("eng", e)] = ctx.enter_context(nc.semaphore(f"se_{e}"))
        for q in ("gpsimd", "sync", "scalar"):
            for s in range(DMA_POOL):
                if self.dma_count[q] > s:
                    sems[("dma", q, s)] = ctx.enter_context(nc.semaphore(f"sd_{q}{s}"))
        self.sems = sems

    def replay(self, eng, h):
        waited = {}
        for op in self.eng_ops[eng]:
            wants = {}
            for d in op.deps:
                if d.is_dma:
                    key = ("dma",) + d.dma_slot
                    val = d.dma_val
                else:
                    key = ("eng", d.eng)
                    val = d.sem_val
                    assert val is not None
                if wants.get(key, -1) < val:
                    wants[key] = val
            for key, val in sorted(wants.items(), key=lambda kv: repr(kv[0])):
                if waited.get(key, -1) >= val:
                    continue
                h.wait_ge(self.sems[key], val)
                waited[key] = val
            inst = op.fn(h)
            if op.is_dma:
                inst.then_inc(self.sems[("dma",) + op.dma_slot], 16)
            elif op.need_inc:
                inst.then_inc(self.sems[("eng", op.eng)], 1)


class Arena:
    PAGE = 1024

    def __init__(self, nc, ctx, nbytes, name="AR"):
        self.t = ctx.enter_context(nc.sbuf_tensor(name, [128, nbytes], mybir.dt.uint8))
        self.nbytes = nbytes

    def buf(self, ofs, nbytes):
        assert ofs % 32 == 0 and ofs + nbytes <= self.nbytes, (ofs, nbytes)
        return Buf(self, ofs, nbytes)


class Buf:
    def __init__(self, arena, ofs, nbytes):
        self.arena, self.ofs, self.nbytes = arena, ofs, nbytes
        p0 = ofs // Arena.PAGE
        p1 = (ofs + nbytes - 1) // Arena.PAGE
        self.tokens = tuple(f"sb{p}" for p in range(p0, p1 + 1))

    def ap(self, dtype, shape=None):
        a = self.arena.t[:, self.ofs:self.ofs + self.nbytes].bitcast(dtype)
        if shape is not None:
            assert shape[0] == 128, shape
            n = 1
            for s in shape[1:]:
                n *= s
            assert n * _dt_size(dtype) <= self.nbytes, (shape, dtype, self.nbytes)
            a = self.arena.t[:, self.ofs:self.ofs + n * _dt_size(dtype)].bitcast(dtype)
            if len(shape) > 2:
                spec = " ".join(f"d{i}" for i in range(1, len(shape)))
                a = a.rearrange(f"p ({spec}) -> p {spec}",
                                **{f"d{i}": shape[i] for i in range(1, len(shape))})
        return a

    def sub(self, ofs, nbytes):
        return Buf(self.arena, self.ofs + ofs, nbytes)


def _dt_size(dt):
    return {F32: 4, BF16: 2}[dt]


def host_prep(inp):
    import ml_dtypes
    BF = ml_dtypes.bfloat16
    x0 = np.ascontiguousarray(np.asarray(inp["L0_attn_x"], np.float32).reshape(SEQ, D))
    BV = np.asarray(inp["L0_attn_block_values"], np.float32)
    cols = np.asarray(inp["L0_attn_cols"], np.int64)
    rows = np.asarray(inp["L0_attn_rows"], np.int64)

    jobs_per_core = []
    for c in range(NCORE):
        jobs = []
        hb1 = 16 * c - 1 if c > 0 else 15
        for i in range(128):
            r = int(rows[i])
            if 16 * c <= r < 16 * c + 16:
                b = r - 16 * c
                jobs.append((i, b // 2, b % 2))
            if r == 0:
                jobs.append((i, 8, 0))
            if r == hb1:
                jobs.append((i, 8, 1))
        jobs_per_core.append(jobs)
    S = max(len(j) for j in jobs_per_core)
    S += S % 2
    S = max(S, 2)
    S2 = S // 2

    xg = np.zeros((NCORE, S, 64, D), np.float32)
    bvt = np.zeros((NCORE, S, 8, 128, 64), np.float32)
    sel = np.zeros((NCORE, S2, 128, NT * 128), np.float32)
    oc_ar = np.arange(64)
    for c in range(NCORE):
        for j, (i, m, h) in enumerate(jobs_per_core[c]):
            xg[c, j] = x0[64 * cols[i]:64 * cols[i] + 64, :]
            bt = BV[i].transpose(1, 2, 0)
            for kp in range(8):
                bvt[c, j, kp, 0:64, :] = bt[:, 2 * kp, :]
                bvt[c, j, kp, 64:128, :] = bt[:, 2 * kp + 1, :]
            sel[c, j // 2, 64 * (j % 2) + oc_ar, 128 * m + 64 * h + oc_ar] = 1.0

    x0c = np.zeros((NCORE, NT, 128, D), np.float32)
    for c in range(NCORE):
        for m in range(NTO):
            x0c[c, m] = x0[TPC * c + 128 * m: TPC * c + 128 * (m + 1)]
        hb1 = 16 * c - 1 if c > 0 else 15
        x0c[c, 8, 0:64] = x0[0:64]
        x0c[c, 8, 64:128] = x0[64 * hb1:64 * hb1 + 64]

    def allowed(i, j):
        return (j <= i) & ((j < 4) | (i - j < 8))

    masks = np.full((NCORE, NTO, 128, 288), NEG, np.float32)
    f128 = np.arange(128)
    f32v = np.arange(32)
    for c in range(NCORE):
        hb1 = 16 * c - 1 if c > 0 else 15
        for t in range(NTO):
            T = NTO * c + t
            iq = (128 * T + f128)[:, None]
            jk = (128 * T + f128)[None, :]
            masks[c, t, :, 0:128] = np.where(allowed(iq, jk), 0.0, NEG)
            if t == 0:
                if c > 0:
                    jk0 = f128.copy()
                    jk0[64:] = 64 * hb1 + f128[64:] - 64
                    masks[c, t, :, 192:256] = np.where(allowed(iq, jk0[None, 64:]), 0.0, NEG)
            else:
                jk = (128 * (T - 1) + f128)[None, :]
                masks[c, t, :, 128:256] = np.where(allowed(iq, jk), 0.0, NEG)
            if T >= 2:
                jk = f32v[None, :]
                masks[c, t, :, 256:288] = np.where(allowed(iq, jk), 0.0, NEG)

    W_up = np.asarray(inp["L0_ffn_W_up"], np.float32)
    W_gate = np.asarray(inp["L0_ffn_W_gate"], np.float32)
    W_down = np.asarray(inp["L0_ffn_W_down"], np.float32)
    W_ug = np.concatenate([np.ascontiguousarray(W_up.T),
                           np.ascontiguousarray(W_gate.T)], axis=1)
    W_ug = np.ascontiguousarray(W_ug.reshape(16, 128, 32, 512).transpose(2, 1, 0, 3)).reshape(32, 128, 8192)
    W_dn = np.ascontiguousarray(W_down.T)
    W_dn = np.ascontiguousarray(W_dn.reshape(32, 2, 128, 2048).transpose(0, 2, 1, 3)).reshape(32, 128, 4096)

    SU_ug = np.asarray(inp["L0_ffn_SU_ug"], np.float32)
    SU_dp = np.asarray(inp["L0_ffn_SU_dp"], np.float32)
    Wsc_ug = np.asarray(inp["L0_ffn_Wscale_ug"], np.float32)
    Wsc_dp = np.asarray(inp["L0_ffn_Wscale_dp"], np.float32)
    su_ug_rep = np.broadcast_to(SU_ug, (128, 2048)).copy()
    su_dp_rep = np.broadcast_to(SU_dp, (128, INTER)).copy()
    wscu_rep = np.broadcast_to(Wsc_ug[:INTER], (128, INTER)).copy()
    wscg_rep = np.broadcast_to(Wsc_ug[INTER:], (128, INTER)).copy()
    wsc_dp_rep = np.broadcast_to(Wsc_dp, (128, 2048)).copy()

    wqT = np.ascontiguousarray(np.asarray(inp["L1_attn_Wq"], np.float32).T)
    wkT = np.ascontiguousarray(np.asarray(inp["L1_attn_Wk"], np.float32).T)
    wvT = np.ascontiguousarray(np.asarray(inp["L1_attn_Wv"], np.float32).T)
    W_qkv = np.concatenate([wqT, wkT, wvT], axis=1)
    W_qkv = np.ascontiguousarray(W_qkv.reshape(16, 128, 8, 512).transpose(2, 1, 0, 3)).reshape(8, 128, 8192)
    woT = np.ascontiguousarray(np.asarray(inp["L1_attn_Wo"], np.float32).T)
    woT = np.ascontiguousarray(woT.reshape(16, 128, 4, 512).transpose(2, 1, 0, 3)).reshape(4, 128, 8192)
    w_gate1 = np.asarray(inp["L1_ffn_w_gate"], np.float32)
    w_up1 = np.asarray(inp["L1_ffn_w_up"], np.float32)
    w_down1 = np.asarray(inp["L1_ffn_w_down"], np.float32)
    W_gu1 = np.concatenate([w_gate1, w_up1], axis=1)
    W_gu1 = np.ascontiguousarray(W_gu1.reshape(16, 128, 32, 512).transpose(2, 1, 0, 3)).reshape(32, 128, 8192)
    w_down1 = np.ascontiguousarray(w_down1.reshape(32, 2, 128, 2048).transpose(0, 2, 1, 3)).reshape(32, 128, 4096)
    wscu_rep = np.ascontiguousarray(wscu_rep.reshape(128, 16, 512).transpose(1, 0, 2))
    wscg_rep = np.ascontiguousarray(wscg_rep.reshape(128, 16, 512).transpose(1, 0, 2))

    H128 = _had(128)
    I128 = np.eye(128, dtype=np.float32)

    bvt = np.ascontiguousarray(
        bvt.reshape(NCORE, S // 2, 2, 8, 128, 64).transpose(0, 1, 4, 2, 3, 5))

    shared = dict(
        W_ug=W_ug, W_dn=W_dn, W_qkv=W_qkv, woT=woT, W_gu1=W_gu1, w_down1=w_down1,
        su_ug_rep=su_ug_rep, su_dp_rep=su_dp_rep,
        wscu_rep=wscu_rep, wscg_rep=wscg_rep, wsc_dp_rep=wsc_dp_rep,
        cH128=H128.astype(BF), cI128=I128.astype(BF),
    )
    percore = [dict(xg=xg[c], bvt=bvt[c], sel=sel[c], x0c=x0c[c], masks=masks[c])
               for c in range(NCORE)]
    return S, shared, percore


def build_program(S, dbg_level=0):
    S2 = S // 2
    nc = bass.Bass("TRN2", target_bir_lowering=False)
    ctx = ExitStack()

    d_in = {}
    def din(name, shape, dt=F32):
        d_in[name] = nc.dram_tensor(name, list(shape), dt, kind="ExternalInput").ap()
        return d_in[name]

    xg_d = din("xg", (S, 64, D))
    bvt_d = din("bvt", (S2, 128, 2, 8, 64))
    sel_d = din("sel", (S2, 128, NT * 128))
    x0c_d = din("x0c", (NT, 128, D))
    masks_d = din("masks", (NTO, 128, 288))
    W_ug_d = din("W_ug", (32, 128, 16 * 512))
    W_dn_d = din("W_dn", (32, 128, 2 * 2048))
    W_qkv_d = din("W_qkv", (8, 128, 16 * 512))
    woT_d = din("woT", (4, 128, 16 * 512))
    W_gu1_d = din("W_gu1", (32, 128, 16 * 512))
    w_down1_d = din("w_down1", (32, 128, 2 * 2048))
    su_ug_rep_d = din("su_ug_rep", (128, 2048))
    su_dp_rep_d = din("su_dp_rep", (128, INTER))
    wscu_rep_d = din("wscu_rep", (16, 128, 512))
    wscg_rep_d = din("wscg_rep", (16, 128, 512))
    wsc_dp_d = din("wsc_dp_rep", (128, 2048))
    H128_d = din("cH128", (128, 128), BF16)
    I128_d = din("cI128", (128, 128), BF16)

    out_d = nc.dram_tensor("out", [TPC, D], F32, kind="ExternalOutput").ap()
    dbg_d = None
    if dbg_level:
        dbg_d = nc.dram_tensor("dbg", [NT, 128, D], F32, kind="ExternalOutput").ap()
    dbg3_d = None
    if dbg_level == 3:
        dbg3_d = nc.dram_tensor("dbg3", [12, 128, D], F32, kind="ExternalOutput").ap()

    PS = [ctx.enter_context(nc.psum_tensor(f"ps{i}", [128, 512], F32))
          for i in range(6)]
    psT = ctx.enter_context(nc.psum_tensor("psT", [128, 1024], BF16))
    psU = ctx.enter_context(nc.psum_tensor("psU", [128, 1024], BF16))

    def ps_ap(i, width=512):
        return PS[i][:, 0:width]

    def ps_tok(i):
        return (f"ps{i}",)

    B = Builder()
    AR = Arena(nc, ctx, 207 * 1024)
    ofs = 0
    def alloc(nbytes, align=64):
        nonlocal ofs
        ofs = (ofs + align - 1) // align * align
        b = AR.buf(ofs, nbytes)
        ofs += nbytes
        return b

    b_const = alloc(2 * 256)
    b_wscdp2 = alloc(8 * 1024)
    b_stats = alloc(2048)
    b_junk = alloc(4096)
    b_masks = alloc(NTO * 288 * 4)
    STREAM = alloc(NT * D * 2)
    OALL = alloc(108 * 1024)
    WBUF = alloc(33 * 1024)
    b_junk2 = alloc(4096)
    assert ofs <= 207 * 1024, ofs

    consts = {
        "I128": (b_const.sub(0, 256), I128_d),
        "H128": (b_const.sub(256, 256), H128_d),
    }

    def cap(name):
        return consts[name][0].ap(BF16, (128, 128))

    def dma(queue, out_ap, in_ap, reads=(), writes=(), tag=""):
        return B.emit(queue, lambda h, o=out_ap, i=in_ap: h.dma_start(out=o, in_=i),
                      reads=reads, writes=writes, dma=True, tag=tag)

    def mm(out_ap, lhsT, rhs, start, stop, reads=(), writes=(), tp=None, tag=""):
        return B.emit("tensor",
                      lambda h, o=out_ap, l=lhsT, r=rhs, s1=start, s2=stop, t=tp:
                          h.matmul(o, l, r, start=s1, stop=s2, tile_position=t),
                      reads=reads, writes=writes, tag=tag)

    def ve(fn, reads=(), writes=(), tag=""):
        return B.emit("vector", fn, reads=reads, writes=writes, tag=tag)

    def sc(fn, reads=(), writes=(), tag=""):
        return B.emit("scalar", fn, reads=reads, writes=writes, tag=tag)

    def x_tile(buf, m):
        return buf.sub(m * 4096, 4096)

    def emit_ln(rb, ob, out_dtype=BF16, tag="ln"):
        rap = rb.ap(BF16, (128, 2048))
        st = b_stats
        sm = st.ap(F32, (128, 512))
        ve(lambda h: h.reduce_sum(sm[:, 0:1], rap, axis=AX.X),
           reads=rb.tokens, writes=st.tokens, tag=tag + "_s")
        sc(lambda h: h.activation(out=b_junk2.ap(BF16, (128, 2048)), in_=rap, func=AF.Square,
                                  accum_out=sm[:, 1:2]),
           reads=rb.tokens, writes=st.tokens + b_junk2.tokens, tag=tag + "_sq")
        ve(lambda h: h.tensor_scalar(sm[:, 2:3], sm[:, 0:1], 1.0 / 2048.0, None, ALU.mult),
           reads=st.tokens, writes=st.tokens, tag=tag + "_m")
        ve(lambda h: h.tensor_tensor(sm[:, 3:4], sm[:, 2:3], sm[:, 2:3], ALU.mult),
           reads=st.tokens, writes=st.tokens)
        ve(lambda h: h.tensor_scalar(sm[:, 4:5], sm[:, 1:2], 1.0 / 2048.0, EPS, ALU.mult, ALU.add),
           reads=st.tokens, writes=st.tokens)
        ve(lambda h: h.tensor_tensor(sm[:, 4:5], sm[:, 4:5], sm[:, 3:4], ALU.subtract),
           reads=st.tokens, writes=st.tokens)
        sc(lambda h: h.activation(out=sm[:, 5:6], in_=sm[:, 4:5], func=AF.Sqrt),
           reads=st.tokens, writes=st.tokens, tag=tag + "_rt")
        ve(lambda h: h.reciprocal(sm[:, 6:7], sm[:, 5:6]),
           reads=st.tokens, writes=st.tokens)
        ve(lambda h: h.tensor_tensor(sm[:, 7:8], sm[:, 2:3], sm[:, 6:7], ALU.mult),
           reads=st.tokens, writes=st.tokens)
        ve(lambda h: h.tensor_scalar(sm[:, 7:8], sm[:, 7:8], -1.0, None, ALU.mult),
           reads=st.tokens, writes=st.tokens)
        oap = ob.ap(out_dtype, (128, 2048))
        sc(lambda h: h.activation(out=oap, in_=rap, func=AF.Identity,
                                  scale=sm[:, 6:7], bias=sm[:, 7:8]),
           reads=rb.tokens + st.tokens, writes=ob.tokens, tag=tag + "_n")

    for name, (b, dap) in consts.items():
        dma("gpsimd", b.ap(BF16, (128, 128)), dap, writes=b.tokens, tag=f"ld_{name}")
    dma("gpsimd", b_wscdp2.ap(F32, (128, 2048)), d_in["wsc_dp_rep"], writes=b_wscdp2.tokens)
    for t in range(NTO):
        dma("gpsimd", b_masks.ap(F32, (128, NTO, 288))[:, t, :], d_in["masks"][t],
            writes=b_masks.tokens, tag=f"ld_mask{t}")
    ve(lambda h: h.tensor_scalar_mul(b_wscdp2.ap(F32, (128, 2048)), b_wscdp2.ap(F32, (128, 2048)), 64.0),
       reads=b_wscdp2.tokens, writes=b_wscdp2.tokens)

    CAP = 12
    XGW = 2080
    xga = WBUF.sub(0, XGW * 2)
    xgb = WBUF.sub(XGW * 2, XGW * 2)
    bvt_db = [WBUF.sub(8320, 2048), WBUF.sub(8320 + 2048, 2048)]
    x0_db = [WBUF.sub(17024, 4096), WBUF.sub(17024 + 4096, 4096)]
    dbgst = WBUF.sub(25216, 8192)
    sel_rb = [OALL.sub(CAP * 4096 + k * 2304, 2304) for k in range(CAP)]

    for xb in (xga, xgb):
        ve(lambda h, a=xb.ap(BF16, (128, XGW)): h.memset(a, 0.0), writes=xb.tokens, tag="xgz")

    n_rounds = (S2 + CAP - 1) // CAP
    for rnd in range(n_rounds):
        q0, q1 = rnd * CAP, min(S2, (rnd + 1) * CAP)
        for q in range(q0, q1):
            sb = sel_rb[q - q0]
            dma("gpsimd", sb.ap(BF16, (128, 1152)), sel_d[q],
                reads=sb.tokens, writes=sb.tokens, tag=f"sel{q}")
            bb = bvt_db[q % 2]
            dma("gpsimd", bb.ap(BF16, (128, 2, 8, 64)), bvt_d[q],
                reads=bb.tokens, writes=bb.tokens, tag=f"bvt{q}")
            for half, s in ((0, 2 * q), (1, 2 * q + 1)):
                xb = xga if half == 0 else xgb
                dma("gpsimd", xb.ap(BF16, (128, XGW))[0:64, 15:15 + 2048], xg_d[s],
                    reads=xb.tokens, writes=xb.tokens, tag=f"xgA{s}")
                dma("gpsimd", xb.ap(BF16, (128, XGW))[64:128, 14:14 + 2048], xg_d[s],
                    reads=xb.tokens, writes=xb.tokens, tag=f"xgB{s}")
            ob = OALL.sub((q - q0) * 4096, 4096)
            for j in range(4):
                pis = ((q % 2) * 3 + (j % 2), (q % 2) * 3 + 2)
                for half, xb in ((0, xga), (1, xgb)):
                    pi = pis[half]
                    pap = ps_ap(pi)
                    for kp in range(8):
                        lhsT = bb.ap(BF16, (128, 2, 8, 64))[:, half, kp, :]
                        rhs = xb.ap(BF16, (128, XGW))[:, 2 * kp + 512 * j: 2 * kp + 512 * j + 512]
                        mm(pap[64 * half:64 * (half + 1), :],
                           lhsT, rhs, start=(kp == 0), stop=(kp == 7),
                           reads=bb.tokens + xb.tokens,
                           writes=ps_tok(pi) if kp in (0, 7) else (),
                           tp=(0, 64 * half), tag=f"cv{q}.{j}.{half}")
                    ve(lambda h, o=ob.ap(BF16, (128, 2048))[64 * half:64 * (half + 1), 512 * j:512 * (j + 1)],
                           p=pap[64 * half:64 * (half + 1), :]:
                           h.tensor_copy(o, p),
                       reads=ps_tok(pi), writes=ob.tokens, tag=f"cvev{q}.{j}.{half}")
        if dbg_level == 3 and rnd == 0:
            for q in range(q0, q1):
                ob = OALL.sub((q - q0) * 4096, 4096)
                ve(lambda h, o=dbgst.ap(F32, (128, 2048)), i=ob.ap(BF16, (128, 2048)):
                       h.tensor_copy(o, i),
                   reads=ob.tokens, writes=dbgst.tokens, tag=f"d3c{q}")
                dma("sync", dbg3_d[q], dbgst.ap(F32, (128, 2048)), reads=dbgst.tokens,
                    writes=(f"dbg3_{q}",), tag=f"d3{q}")
        last_round = (rnd == n_rounds - 1)
        for m in range(NT):
            xb0 = x0_db[m % 2]
            if last_round:
                dma("gpsimd", xb0.ap(BF16, (128, 2048)), x0c_d[m],
                    reads=xb0.tokens, writes=xb0.tokens, tag=f"x0bf{m}")
            for j in range(4):
                pi = (m % 2) * 3 + (j % 2)
                pap = ps_ap(pi)
                for q in range(q0, q1):
                    sb = sel_rb[q - q0]
                    ob = OALL.sub((q - q0) * 4096, 4096)
                    st_f = (q == q0)
                    sp_f = (not last_round and q == q1 - 1)
                    mm(pap, sb.ap(BF16, (128, 1152))[:, 128 * m:128 * (m + 1)],
                       ob.ap(BF16, (128, 2048))[:, 512 * j:512 * (j + 1)],
                       start=st_f, stop=sp_f,
                       reads=sb.tokens + ob.tokens,
                       writes=ps_tok(pi) if (st_f or sp_f) else (), tag=f"sel{m}.{j}")
                if last_round:
                    mm(pap, cap("I128"), xb0.ap(BF16, (128, 2048))[:, 512 * j:512 * (j + 1)],
                       start=False, stop=True,
                       reads=consts["I128"][0].tokens + xb0.tokens, writes=ps_tok(pi),
                       tag=f"selr{m}.{j}")
                rb = x_tile(STREAM, m)
                if rnd == 0:
                    ve(lambda h, o=rb.ap(BF16, (128, 2048))[:, 512 * j:512 * (j + 1)], p=pap:
                           h.tensor_copy(o, p),
                       reads=ps_tok(pi), writes=rb.tokens, tag=f"selev{m}.{j}")
                else:
                    ve(lambda h, o=rb.ap(BF16, (128, 2048))[:, 512 * j:512 * (j + 1)], p=pap:
                           h.tensor_tensor(o, o, p, ALU.add),
                       reads=ps_tok(pi) + rb.tokens, writes=rb.tokens, tag=f"selac{m}.{j}")

    ACT = OALL
    for m in range(NT):
        emit_ln(x_tile(STREAM, m), x_tile(ACT, m), tag=f"ln1.{m}")

    def dump_and_exit(ntile):
        fstage = b_masks.sub(0, 8192)
        fence_tokens = []
        for m in range(ntile):
            xb = x_tile(ACT, m)
            ve(lambda h, o=fstage.ap(F32, (128, 2048)), i=xb.ap(BF16, (128, 2048)):
               h.tensor_copy(o, i),
               reads=xb.tokens, writes=fstage.tokens, tag=f"dbgc{m}")
            dma("sync", dbg_d[m], fstage.ap(F32, (128, 2048)), reads=fstage.tokens,
                writes=(f"dbg{m}",), tag=f"dbg{m}")
            fence_tokens.append(f"dbg{m}")
        B.emit("sync", lambda h: h.nop(), reads=tuple(fence_tokens), tag="fence")
        B.finalize(nc, ctx)
        _replay(nc, B)
        ctx.close()
        return nc

    if dbg_level == 1:
        return dump_and_exit(NT)

    def park(name, src_d, nslab, wcols):
        pd = nc.dram_tensor(name, [nslab, 128, wcols], BF16, kind="Internal").ap()
        for cs2 in range(nslab):
            dma("gpsimd", pd[cs2], src_d[cs2], writes=(name,), tag=f"pk_{name}{cs2}")
        return pd

    P_ug = park("P_ug", W_ug_d, 32, 8192)
    P_dn = park("P_dn", W_dn_d, 32, 4096)
    P_qkv = park("P_qkv", W_qkv_d, 8, 8192)
    P_wo = park("P_wo", woT_d, 4, 8192)
    P_gu1 = park("P_gu1", W_gu1_d, 32, 8192)
    P_dn1 = park("P_dn1", w_down1_d, 32, 4096)

    OALL2 = OALL.sub(36864, 73728)
    Hc = cap("H128")
    Ic = cap("I128")

    def transpose_tile(src_buf_ap, dst_buf, nchunk, tag, np_src=None):
        for r in range((nchunk + 3) // 4):
            pst = psT if r % 2 == 0 else psU
            ptok = ("psT",) if r % 2 == 0 else ("psU",)
            c0, c1 = 4 * r, min(nchunk, 4 * r + 4)
            for c in range(c0, c1):
                B.emit("tensor",
                       lambda h, o=pst[:, (c % 4) * 128:(c % 4) * 128 + 128],
                       i=src_buf_ap[:, c * 128:c * 128 + 128]:
                       h.transpose(o, i, Ic),
                       reads=(np_src or ()) + consts["I128"][0].tokens,
                       writes=ptok, tag=f"{tag}_t{c}")
            ve(lambda h, o=dst_buf.ap(BF16, (128, nchunk * 128))[:, c0 * 128:c1 * 128],
               p=pst[:, 0:(c1 - c0) * 128]: h.tensor_copy(o, p),
               reads=ptok, writes=dst_buf.tokens, tag=f"{tag}_e{r}")

    def butterfly(nch, buf_a, buf_b, tag):
        cur, oth = buf_a, buf_b
        w = nch * 128
        s = nch // 2
        while s >= 1:
            blk = nch // (2 * s)
            si = s * 128
            ia = cur.ap(BF16, (128, blk, 2, si))
            oa = oth.ap(BF16, (128, blk, 2, si))
            ve(lambda h, o=oa[:, :, 0, :], x=ia[:, :, 0, :], y=ia[:, :, 1, :]:
               h.tensor_tensor(o, x, y, ALU.add),
               reads=cur.tokens, writes=oth.tokens, tag=f"{tag}_p{s}")
            ve(lambda h, o=oa[:, :, 1, :], x=ia[:, :, 0, :], y=ia[:, :, 1, :]:
               h.tensor_tensor(o, x, y, ALU.subtract),
               reads=cur.tokens, writes=oth.tokens, tag=f"{tag}_m{s}")
            cur, oth = oth, cur
            s //= 2
        return cur

    su_ug_sb = WBUF.sub(0, 4096)
    su_dp_sb = WBUF.sub(4096, 16384)
    pp_db = [WBUF.sub(20480, 4096), WBUF.sub(24576, 4096)]
    wsc_db = [WBUF.sub(20480, 2048), WBUF.sub(22528, 2048)]
    wdn_db = [WBUF.sub(20480, 4096), WBUF.sub(24576, 4096)]
    tb_db = [WBUF.sub(28672, 1024), WBUF.sub(29696, 1024)]
    gt_t = [WBUF.sub(24576, 2048), WBUF.sub(26624, 2048), WBUF.sub(30720, 2048)]
    wslab_db = [STREAM.sub(0, 16384), STREAM.sub(16384, 16384)]
    r_b = STREAM.sub(32768, 4096)

    dma("gpsimd", su_ug_sb.ap(BF16, (128, 2048)), d_in["su_ug_rep"],
        writes=su_ug_sb.tokens, tag="ld_suug")
    dma("gpsimd", su_dp_sb.ap(BF16, (128, INTER)), d_in["su_dp_rep"],
        writes=su_dp_sb.tokens, tag="ld_sudp")

    C8192P = float(1.0 / np.sqrt(np.float32(8192.0)))

    F0_GROUPS = [[0, 1, 2], [3, 4, 5], [6, 7, 8]]
    xsT = [OALL2.sub(t * 4096, 4096) for t in range(3)]
    xhT = [OALL2.sub(12288 + t * 4096, 4096) for t in range(3)]
    h_b = [OALL2.sub(24576 + t * 16384, 16384) for t in range(3)]
    sg_b = [b_junk.sub(t * 1024, 1024) for t in range(3)]

    for grp in F0_GROUPS:
        for ti, t in enumerate(grp):
            xa = x_tile(ACT, t)
            ve(lambda h, o=pp_db[0].ap(BF16, (128, 2048)), x=xa.ap(BF16, (128, 2048)),
               s=su_ug_sb.ap(BF16, (128, 2048)): h.tensor_tensor(o, x, s, ALU.mult),
               reads=xa.tokens + su_ug_sb.tokens, writes=pp_db[0].tokens, tag=f"f0su{t}")
            fin = butterfly(16, pp_db[0], pp_db[1], f"f0bf{t}")
            transpose_tile(fin.ap(BF16, (128, 2048)), xsT[ti], 16, f"f0T{t}",
                           np_src=fin.tokens)
            for j in range(4):
                pi = j % 2
                mm(ps_ap(pi), Hc, xsT[ti].ap(BF16, (128, 2048))[:, 512 * j:512 * (j + 1)],
                   start=True, stop=True,
                   reads=consts["H128"][0].tokens + xsT[ti].tokens,
                   writes=ps_tok(pi), tag=f"f0H{t}.{j}")
                sc(lambda h, o=xhT[ti].ap(BF16, (128, 2048))[:, 512 * j:512 * (j + 1)],
                   p=ps_ap(pi): h.activation(out=o, in_=p, func=AF.Identity, scale=C2048),
                   reads=ps_tok(pi), writes=xhT[ti].tokens, tag=f"f0He{t}.{j}")
        for s in range(16):
            for half, cs in ((0, 16 + s), (1, s)):
                wb = wslab_db[(2 * s + half) % 2]
                dma("sync", wb.ap(BF16, (128, 16 * 512)), P_ug[cs],
                    reads=wb.tokens + ("P_ug",), writes=wb.tokens, tag=f"f0w{s}.{half}")
                wsb = wsc_db[(2 * s + half) % 2]
                dma("scalar", wsb.ap(F32, (128, 512)),
                    (d_in["wscg_rep"] if half == 0 else d_in["wscu_rep"])[s],
                    reads=wsb.tokens, writes=wsb.tokens, tag=f"f0ws{s}.{half}")
                for ti, t in enumerate(grp):
                    pi = ti
                    for a in range(16):
                        mm(ps_ap(pi), xhT[ti].ap(BF16, (128, 2048))[:, a * 128:(a + 1) * 128],
                           wb.ap(BF16, (128, 16, 512))[:, a, :],
                           start=(a == 0), stop=(a == 15),
                           reads=xhT[ti].tokens + wb.tokens,
                           writes=ps_tok(pi) if a in (0, 15) else (),
                           tag=f"f0m{s}.{half}.{t}")
                    gtb = gt_t[ti]
                    if half == 0:
                        ve(lambda h, o=gtb.ap(F32, (128, 512)), p=ps_ap(pi),
                           w=wsb.ap(F32, (128, 512)): h.tensor_tensor(o, p, w, ALU.mult),
                           reads=ps_tok(pi) + wsb.tokens, writes=gtb.tokens, tag=f"f0g{s}.{t}")
                        sc(lambda h, o=sg_b[ti].ap(BF16, (128, 512)), i=gtb.ap(F32, (128, 512)):
                           h.activation(out=o, in_=i, func=AF.Silu, scale=64.0),
                           reads=gtb.tokens, writes=sg_b[ti].tokens, tag=f"f0sg{s}.{t}")
                    else:
                        ve(lambda h, o=gtb.ap(F32, (128, 512)), p=ps_ap(pi),
                           w=wsb.ap(F32, (128, 512)): h.tensor_tensor(o, p, w, ALU.mult),
                           reads=ps_tok(pi) + wsb.tokens, writes=gtb.tokens, tag=f"f0u{s}.{t}")
                        ve(lambda h, o=h_b[ti].ap(BF16, (128, INTER))[:, 512 * s:512 * (s + 1)],
                           x=gtb.ap(F32, (128, 512)), y=sg_b[ti].ap(BF16, (128, 512)):
                           h.tensor_tensor(o, x, y, ALU.mult),
                           reads=gtb.tokens + sg_b[ti].tokens, writes=h_b[ti].tokens,
                           tag=f"f0h{s}.{t}")
        for ti, t in enumerate(grp):
            hb = h_b[ti]
            ve(lambda h, o=hb.ap(BF16, (128, INTER)), s=su_dp_sb.ap(BF16, (128, INTER)):
               h.tensor_tensor(o, o, s, ALU.mult),
               reads=hb.tokens + su_dp_sb.tokens, writes=hb.tokens, tag=f"f0sd{t}")
            pp8 = STREAM.sub(0, 16384)
            fin = butterfly(64, hb, pp8, f"f0b8{t}")
            xwT = STREAM.sub(16384, 16384)
            for q in range(16):
                pst = psT if q % 2 == 0 else psU
                ptok = ("psT",) if q % 2 == 0 else ("psU",)
                for j in range(4):
                    c = 4 * q + j
                    B.emit("tensor",
                           lambda h, o=pst[:, j * 128:(j + 1) * 128],
                           i=fin.ap(BF16, (128, INTER))[:, c * 128:(c + 1) * 128]:
                           h.transpose(o, i, Ic),
                           reads=fin.tokens + consts["I128"][0].tokens,
                           writes=ptok, tag=f"f0dT{t}.{c}")
                tb = tb_db[q % 2]
                ve(lambda h, o=tb.ap(BF16, (128, 512)), p=pst[:, 0:512]: h.tensor_copy(o, p),
                   reads=ptok, writes=tb.tokens, tag=f"f0dTe{t}.{q}")
                pi = 4 + (q % 2)
                mm(ps_ap(pi), Hc, tb.ap(BF16, (128, 512)), start=True, stop=True,
                   reads=consts["H128"][0].tokens + tb.tokens, writes=ps_tok(pi),
                   tag=f"f0dH{t}.{q}")
                sc(lambda h, o=xwT.ap(BF16, (128, INTER))[:, 512 * q:512 * (q + 1)],
                   p=ps_ap(pi): h.activation(out=o, in_=p, func=AF.Identity, scale=C8192P),
                   reads=ps_tok(pi), writes=xwT.tokens, tag=f"f0dHe{t}.{q}")
            wdn2_db = [hb.sub(0, 8192), hb.sub(8192, 8192), OALL2.sub(12288, 8192)]
            for cp in range(32):
                wd = wdn2_db[cp % 3]
                dma("gpsimd", wd.ap(BF16, (128, 4096)), P_dn[cp],
                    reads=wd.tokens + ("P_dn",), writes=wd.tokens, tag=f"f0dw{t}.{cp}")
                for k in range(2):
                    c = 2 * cp + k
                    for j in range(4):
                        mm(ps_ap(j), xwT.ap(BF16, (128, INTER))[:, c * 128:(c + 1) * 128],
                           wd.ap(BF16, (128, 2, 2048))[:, k, 512 * j:512 * (j + 1)],
                           start=(c == 0), stop=(c == 63),
                           reads=xwT.tokens + wd.tokens,
                           writes=ps_tok(j) if c in (0, 63) else (),
                           tag=f"f0dm{t}.{c}.{j}")
            xa = x_tile(ACT, t)
            rb0 = xsT[ti]
            for j in range(4):
                gtb = gt_t[j % 2]
                ve(lambda h, o=gtb.ap(F32, (128, 512)), p=ps_ap(j),
                   w=b_wscdp2.ap(F32, (128, 2048))[:, 512 * j:512 * (j + 1)]:
                   h.tensor_tensor(o, p, w, ALU.mult),
                   reads=ps_tok(j) + b_wscdp2.tokens, writes=gtb.tokens, tag=f"f0r{t}.{j}")
                ve(lambda h, o=rb0.ap(BF16, (128, 2048))[:, 512 * j:512 * (j + 1)],
                   x=gtb.ap(F32, (128, 512)), y=xa.ap(BF16, (128, 2048))[:, 512 * j:512 * (j + 1)]:
                   h.tensor_tensor(o, x, y, ALU.add),
                   reads=gtb.tokens + xa.tokens, writes=rb0.tokens, tag=f"f0ra{t}.{j}")
            emit_ln(rb0, x_tile(ACT, t), tag=f"ln2.{t}")

    if dbg_level == 4:
        return dump_and_exit(NT)

    Qbuf = [OALL2.sub(t * 4096, 4096) for t in range(2)]
    Kbuf = [OALL2.sub(8192 + t * 2048, 2048) for t in range(2)]
    KT_st = [OALL2.sub(12288 + t * 2048, 2048) for t in range(NT)]
    V_st = [OALL2.sub(30720 + t * 2048, 2048) for t in range(NT)]
    QT = [OALL2.sub(49152 + t * 4096, 4096) for t in range(2)]
    attnT = [OALL2.sub(57344 + t * 4096, 4096) for t in range(2)]
    X2T = [OALL2.sub(65536 + t * 4096, 4096) for t in range(2)]
    _scb = [b_junk, STREAM.sub(32768, 4096)]
    sc_s = [b.sub(0, 1152) for b in _scb]
    sc_e = [b.sub(1152, 576) for b in _scb]
    sc_a = [b.sub(1728, 576) for b in _scb]
    sc_at = [b.sub(2304, 768) for b in _scb]
    sc_r = [b.sub(3072, 32) for b in _scb]
    wA_db = [STREAM.sub(0, 16384), STREAM.sub(16384, 16384)]
    wo_db = [WBUF.sub(0, 16384), WBUF.sub(16384, 16384)]
    rb_of = lambda ti: Qbuf[ti]

    A1_GROUPS = [[8, 0], [1, 2], [3, 4], [5, 6], [7]]

    def prev_tile(t):
        return 8 if t == 0 else t - 1

    for grp in A1_GROUPS:
        for ti, t in enumerate(grp):
            transpose_tile(x_tile(ACT, t).ap(BF16, (128, 2048)), X2T[ti], 16, f"a1xT{t}",
                           np_src=x_tile(ACT, t).tokens)
        for s in range(8):
            wb = wA_db[s % 2]
            dma("scalar", wb.ap(BF16, (128, 16 * 512)), P_qkv[s],
                reads=wb.tokens + ("P_qkv",), writes=wb.tokens, tag=f"a1w{s}")
            for ti, t in enumerate(grp):
                pi = ti % 2
                for a in range(16):
                    mm(ps_ap(pi), X2T[ti].ap(BF16, (128, 2048))[:, a * 128:(a + 1) * 128],
                       wb.ap(BF16, (128, 16, 512))[:, a, :],
                       start=(a == 0), stop=(a == 15),
                       reads=X2T[ti].tokens + wb.tokens,
                       writes=ps_tok(pi) if a in (0, 15) else (),
                       tag=f"a1m{s}.{t}")
                if s < 4:
                    dst, off, scl = Qbuf[ti], 512 * s, ISCALE
                elif s < 6:
                    dst, off, scl = Kbuf[ti], 512 * (s - 4), 1.0
                else:
                    dst, off, scl = V_st[t], 512 * (s - 6), 1.0
                sc(lambda h, o=dst.ap(BF16, (128, dst.nbytes // 2))[:, off:off + 512],
                   p=ps_ap(pi), sv=scl: h.activation(out=o, in_=p, func=AF.Identity, scale=sv),
                   reads=ps_tok(pi), writes=dst.tokens, tag=f"a1e{s}.{t}")
        for ti, t in enumerate(grp):
            transpose_tile(Qbuf[ti].ap(BF16, (128, 2048)), QT[ti], 16, f"a1qT{t}",
                           np_src=Qbuf[ti].tokens)
            transpose_tile(Kbuf[ti].ap(BF16, (128, 1024)), KT_st[t], 8, f"a1kT{t}",
                           np_src=Kbuf[ti].tokens)
        for ti, t in enumerate(grp):
            if t == 8:
                continue
            tp = prev_tile(t)
            mask_ap = b_masks.ap(F32, (128, NTO, 288))[:, t, :]
            for qh in range(16):
                kv = qh // 2
                par = qh % 2
                scs, sce, sca, scat, scr = (sc_s[par], sc_e[par], sc_a[par],
                                            sc_at[par], sc_r[par])
                pst, ptok = (psT, ("psT",)) if par == 0 else (psU, ("psU",))
                pi = 2 + par
                po = 4 + par
                lq = QT[ti].ap(BF16, (128, 2048))[:, qh * 128:(qh + 1) * 128]
                segs = [
                    (KT_st[t].ap(BF16, (128, 1024))[:, kv * 128:(kv + 1) * 128], 0, 128),
                    (KT_st[tp].ap(BF16, (128, 1024))[:, kv * 128:(kv + 1) * 128], 128, 128),
                    (KT_st[8].ap(BF16, (128, 1024))[:, kv * 128:kv * 128 + 32], 256, 32),
                ]
                for (kap, off, w) in segs:
                    mm(PS[pi][:, off:off + w], lq, kap, start=True, stop=True,
                       reads=QT[ti].tokens + KT_st[t].tokens + KT_st[tp].tokens + KT_st[8].tokens,
                       writes=ps_tok(pi), tag=f"a1s{t}.{qh}")
                ve(lambda h, o=scs.ap(F32, (128, 288)), p=PS[pi][:, 0:288], mk=mask_ap:
                   h.tensor_tensor(o, p, mk, ALU.add),
                   reads=ps_tok(pi) + b_masks.tokens, writes=scs.tokens, tag=f"a1sm{t}.{qh}")
                sc(lambda h, o=sce.ap(BF16, (128, 288)), i=scs.ap(F32, (128, 288)),
                   a=scr.ap(F32, (128, 8))[:, 0:1]:
                   h.activation(out=o, in_=i, func=AF.Exp, accum_out=a),
                   reads=scs.tokens, writes=sce.tokens + scr.tokens, tag=f"a1ex{t}.{qh}")
                ve(lambda h, o=scr.ap(F32, (128, 8))[:, 1:2], i=scr.ap(F32, (128, 8))[:, 0:1]:
                   h.reciprocal(o, i),
                   reads=scr.tokens, writes=scr.tokens, tag=f"a1rc{t}.{qh}")
                sc(lambda h, o=sca.ap(BF16, (128, 288)), i=sce.ap(BF16, (128, 288)),
                   sv=scr.ap(F32, (128, 8))[:, 1:2]:
                   h.activation(out=o, in_=i, func=AF.Identity, scale=sv),
                   reads=sce.tokens + scr.tokens, writes=sca.tokens, tag=f"a1aw{t}.{qh}")
                for sgi, (off, w) in enumerate(((0, 128), (128, 128), (256, 32))):
                    B.emit("tensor",
                           lambda h, o=pst[0:w, sgi * 128:sgi * 128 + 128],
                           i=sca.ap(BF16, (128, 288))[:, off:off + w]:
                           h.transpose(o, i, Ic),
                           reads=sca.tokens + consts["I128"][0].tokens,
                           writes=ptok, tag=f"a1at{t}.{qh}.{sgi}")
                ve(lambda h, o=scat.ap(BF16, (128, 384)), p=pst[:, 0:384]: h.tensor_copy(o, p),
                   reads=ptok, writes=scat.tokens, tag=f"a1ae{t}.{qh}")
                vsegs = [
                    (V_st[t].ap(BF16, (128, 1024))[:, kv * 128:(kv + 1) * 128], 0, 128),
                    (V_st[tp].ap(BF16, (128, 1024))[:, kv * 128:(kv + 1) * 128], 1, 128),
                    (V_st[8].ap(BF16, (128, 1024))[0:32, kv * 128:(kv + 1) * 128], 2, 32),
                ]
                for (vap, sgi, w) in vsegs:
                    mm(PS[po][:, 0:128], vap,
                       scat.ap(BF16, (128, 384))[0:w, sgi * 128:sgi * 128 + 128],
                       start=(sgi == 0), stop=(sgi == 2),
                       reads=scat.tokens + V_st[t].tokens + V_st[tp].tokens + V_st[8].tokens,
                       writes=ps_tok(po) if sgi in (0, 2) else (),
                       tag=f"a1o{t}.{qh}.{sgi}")
                ve(lambda h, o=attnT[ti].ap(BF16, (128, 2048))[:, qh * 128:(qh + 1) * 128],
                   p=PS[po][:, 0:128]: h.tensor_copy(o, p),
                   reads=ps_tok(po), writes=attnT[ti].tokens, tag=f"a1oe{t}.{qh}")
        for os_ in range(4):
            wb = wo_db[os_ % 2]
            dma("scalar", wb.ap(BF16, (128, 16 * 512)), P_wo[os_],
                reads=wb.tokens + ("P_wo",), writes=wb.tokens, tag=f"a1wo{os_}")
            for ti, t in enumerate(grp):
                if t == 8:
                    continue
                pi = ti % 2
                for a in range(16):
                    mm(ps_ap(pi), attnT[ti].ap(BF16, (128, 2048))[:, a * 128:(a + 1) * 128],
                       wb.ap(BF16, (128, 16, 512))[:, a, :],
                       start=(a == 0), stop=(a == 15),
                       reads=attnT[ti].tokens + wb.tokens,
                       writes=ps_tok(pi) if a in (0, 15) else (),
                       tag=f"a1om{os_}.{t}")
                xa = x_tile(ACT, t)
                ve(lambda h, o=rb_of(ti).ap(BF16, (128, 2048))[:, 512 * os_:512 * (os_ + 1)],
                   p=ps_ap(pi), x=xa.ap(BF16, (128, 2048))[:, 512 * os_:512 * (os_ + 1)]:
                   h.tensor_tensor(o, p, x, ALU.add),
                   reads=ps_tok(pi) + xa.tokens, writes=rb_of(ti).tokens, tag=f"a1r{os_}.{t}")
        for ti, t in enumerate(grp):
            if t == 8:
                continue
            emit_ln(rb_of(ti), x_tile(ACT, t), tag=f"ln3.{t}")

    if dbg_level == 5:
        return dump_and_exit(NTO)

    h1_b = [OALL2.sub(t * 16384, 16384) for t in range(2)]
    hsT = [OALL2.sub(32768 + t * 16384, 16384) for t in range(2)]
    X3T = [OALL2.sub(65536 + t * 4096, 4096) for t in range(2)]
    A4_b = WBUF.sub(0, 4096)
    mn_b = [WBUF.sub(4096 + i * 1024, 1024) for i in range(4)]
    t2_b = WBUF.sub(8192, 1024)
    msk_b = WBUF.sub(9216, 2048)
    sgf_b = WBUF.sub(11264, 2048)
    wdn1_db = [WBUF.sub(20480, 4096), WBUF.sub(24576, 4096)]
    wg_db = [STREAM.sub(0, 16384), STREAM.sub(16384, 16384)]
    r1_b = STREAM.sub(32768, 4096)
    fstage = b_masks.sub(0, 8192)

    F1_GROUPS = [[0, 1], [2, 3], [4, 5], [6, 7]]
    out_tokens = []
    for grp in F1_GROUPS:
        for ti, t in enumerate(grp):
            transpose_tile(x_tile(ACT, t).ap(BF16, (128, 2048)), X3T[ti], 16, f"f1xT{t}",
                           np_src=x_tile(ACT, t).tokens)
        for s in range(16):
            for half in (0, 1):
                cs = s + half * 16
                wb = wg_db[(2 * s + half) % 2]
                dma("sync" if (2 * s + half) % 2 == 0 else "scalar",
                    wb.ap(BF16, (128, 16 * 512)), P_gu1[cs],
                    reads=wb.tokens + ("P_gu1",), writes=wb.tokens, tag=f"f1w{s}.{half}")
                for ti, t in enumerate(grp):
                    pi = ti % 2
                    for a in range(16):
                        mm(ps_ap(pi), X3T[ti].ap(BF16, (128, 2048))[:, a * 128:(a + 1) * 128],
                           wb.ap(BF16, (128, 16, 512))[:, a, :],
                           start=(a == 0), stop=(a == 15),
                           reads=X3T[ti].tokens + wb.tokens,
                           writes=ps_tok(pi) if a in (0, 15) else (),
                           tag=f"f1m{s}.{half}.{t}")
                    if half == 0:
                        sc(lambda h, o=sg_b[ti].ap(BF16, (128, 512)), p=ps_ap(pi):
                           h.activation(out=o, in_=p, func=AF.Silu),
                           reads=ps_tok(pi), writes=sg_b[ti].tokens, tag=f"f1sg{s}.{t}")
                    else:
                        ve(lambda h, o=h1_b[ti].ap(BF16, (128, INTER))[:, 512 * s:512 * (s + 1)],
                           p=ps_ap(pi), g=sg_b[ti].ap(BF16, (128, 512)):
                           h.tensor_tensor(o, p, g, ALU.mult),
                           reads=ps_tok(pi) + sg_b[ti].tokens, writes=h1_b[ti].tokens,
                           tag=f"f1h{s}.{t}")
        for ti, t in enumerate(grp):
            hb = h1_b[ti]
            for sec in range(4):
                hsec = hb.ap(BF16, (128, INTER))[:, 2048 * sec:2048 * (sec + 1)]
                sc(lambda h, o=A4_b.ap(BF16, (128, 2048)), i=hsec:
                   h.activation(out=o, in_=i, func=AF.Abs),
                   reads=hb.tokens, writes=A4_b.tokens, tag=f"f1ab{t}.{sec}")
                g0 = A4_b.ap(BF16, (128, 512, 4))
                for i, (x, y, op) in enumerate(((0, 1, ALU.max), (0, 1, ALU.min),
                                                (2, 3, ALU.max), (2, 3, ALU.min))):
                    ve(lambda h, o=mn_b[i].ap(BF16, (128, 512)), a1=g0[:, :, x], a2=g0[:, :, y],
                       op_=op: h.tensor_tensor(o, a1, a2, op_),
                       reads=A4_b.tokens, writes=mn_b[i].tokens, tag=f"f1mn{t}.{sec}.{i}")
                ve(lambda h, o=mn_b[0].ap(BF16, (128, 512)), a1=mn_b[0].ap(BF16, (128, 512)),
                   a2=mn_b[2].ap(BF16, (128, 512)): h.tensor_tensor(o, a1, a2, ALU.min),
                   reads=mn_b[0].tokens + mn_b[2].tokens, writes=mn_b[0].tokens,
                   tag=f"f1c1{t}.{sec}")
                ve(lambda h, o=mn_b[1].ap(BF16, (128, 512)), a1=mn_b[1].ap(BF16, (128, 512)),
                   a2=mn_b[3].ap(BF16, (128, 512)): h.tensor_tensor(o, a1, a2, ALU.max),
                   reads=mn_b[1].tokens + mn_b[3].tokens, writes=mn_b[1].tokens,
                   tag=f"f1c2{t}.{sec}")
                ve(lambda h, o=t2_b.ap(BF16, (128, 512)), a1=mn_b[0].ap(BF16, (128, 512)),
                   a2=mn_b[1].ap(BF16, (128, 512)): h.tensor_tensor(o, a1, a2, ALU.max),
                   reads=mn_b[0].tokens + mn_b[1].tokens, writes=t2_b.tokens,
                   tag=f"f1t2{t}.{sec}")
                hv = hb.ap(BF16, (128, INTER)).rearrange(
                    "p (ss g four) -> p ss g four", ss=4, four=4)
                for i in range(4):
                    ve(lambda h, o=msk_b.ap(F32, (128, 512)), a1=g0[:, :, i],
                       a2=t2_b.ap(BF16, (128, 512)): h.tensor_tensor(o, a1, a2, ALU.is_ge),
                       reads=A4_b.tokens + t2_b.tokens, writes=msk_b.tokens,
                       tag=f"f1ge{t}.{sec}.{i}")
                    ve(lambda h, o=hv[:, sec, :, i], mk=msk_b.ap(F32, (128, 512)):
                       h.tensor_tensor(o, o, mk, ALU.mult),
                       reads=hb.tokens + msk_b.tokens, writes=hb.tokens,
                       tag=f"f1ap{t}.{sec}.{i}")
            transpose_tile(hb.ap(BF16, (128, INTER)), hsT[ti], 64, f"f1hT{t}",
                           np_src=hb.tokens)
            wdn3_db = [hb.sub(0, 8192), hb.sub(8192, 8192), b_masks.sub(0, 8192)]
            for cp in range(32):
                wd = wdn3_db[cp % 3]
                dma("gpsimd" if cp % 2 == 0 else "scalar",
                    wd.ap(BF16, (128, 4096)), P_dn1[cp],
                    reads=wd.tokens + ("P_dn1",), writes=wd.tokens, tag=f"f1dw{t}.{cp}")
                for k in range(2):
                    c = 2 * cp + k
                    for j in range(4):
                        mm(ps_ap(j), hsT[ti].ap(BF16, (128, INTER))[:, c * 128:(c + 1) * 128],
                           wd.ap(BF16, (128, 2, 2048))[:, k, 512 * j:512 * (j + 1)],
                           start=(c == 0), stop=(c == 63),
                           reads=hsT[ti].tokens + wd.tokens,
                           writes=ps_tok(j) if c in (0, 63) else (),
                           tag=f"f1dm{t}.{c}.{j}")
            xa = x_tile(ACT, t)
            rb1 = X3T[ti]
            for j in range(4):
                ve(lambda h, o=rb1.ap(BF16, (128, 2048))[:, 512 * j:512 * (j + 1)],
                   p=ps_ap(j), x=xa.ap(BF16, (128, 2048))[:, 512 * j:512 * (j + 1)]:
                   h.tensor_tensor(o, p, x, ALU.add),
                   reads=ps_tok(j) + xa.tokens, writes=rb1.tokens, tag=f"f1r{t}.{j}")
            emit_ln(rb1, x_tile(ACT, t), tag=f"ln4.{t}")
            ve(lambda h, o=fstage.ap(F32, (128, 2048)), i=x_tile(ACT, t).ap(BF16, (128, 2048)):
               h.tensor_copy(o, i),
               reads=x_tile(ACT, t).tokens, writes=fstage.tokens, tag=f"fout{t}")
            dma("sync", out_d[128 * t:128 * (t + 1), :], fstage.ap(F32, (128, 2048)),
                reads=fstage.tokens, writes=(f"out{t}",), tag=f"out{t}")
            out_tokens.append(f"out{t}")

    B.emit("sync", lambda h: h.nop(), reads=tuple(out_tokens), tag="fence")
    B.finalize(nc, ctx)
    _replay(nc, B)
    ctx.close()
    return nc


def _replay(nc, B):
    with nc.Block() as block:
        @block.tensor
        def _(h):
            B.replay("tensor", h)

        @block.vector
        def _(h):
            B.replay("vector", h)

        @block.scalar
        def _(h):
            B.replay("scalar", h)

        @block.gpsimd
        def _(h):
            B.replay("gpsimd", h)

        @block.sync
        def _(h):
            B.replay("sync", h)


def kernel(**inputs):
    S, shared, percore = host_prep(inputs)
    nc = build_program(S, dbg_level=0)
    in_maps = []
    for c in range(NCORE):
        m = dict(shared)
        m.update(percore[c])
        in_maps.append(m)
    res = run_bass_kernel_spmd(nc, in_maps, core_ids=list(range(NCORE)))
    out = np.concatenate([res.results[c]["out"] for c in range(NCORE)], axis=0)
    return out.reshape(1, SEQ, D).astype(np.float32)
